# version 10
# baseline (speedup 1.0000x reference)
import os
from contextlib import ExitStack

import numpy as np

import concourse.bass as bass
import concourse.mybir as mybir
from concourse.bass_utils import run_bass_kernel_spmd

F32 = mybir.dt.float32
BF16 = mybir.dt.bfloat16
AF = mybir.ActivationFunctionType
ALU = mybir.AluOpType
AX = mybir.AxisListType

S = 2048
D = 1024
NT = 16
NP = 8
NCORE = 8


def build(stage=4, debug=()):
    nc = bass.Bass("TRN2", target_bir_lowering=False)

    def dram_in(name, shape):
        return nc.dram_tensor(name, list(shape), F32, kind="ExternalInput")

    x_d = dram_in("x", [S, D])
    pqk_d = dram_in("pqk_w", [D, D])
    pv_d = dram_in("pv_w", [D, D])
    ipw_d = dram_in("ipw", [3 * D, D])
    opw_d = dram_in("opw", [D, D])
    wq_d = dram_in("Wq", [D, D])
    wk_d = dram_in("Wk", [D, D])
    wv_d = dram_in("Wv", [D, D])
    wo_d = dram_in("Wo", [D, D])
    w1_d = dram_in("w1", [8, D, 4096])
    w2_d = dram_in("w2", [8, 4096, D])
    b2_d = dram_in("b2", [8, D])
    ident_d = dram_in("ident", [128, 128])
    mska_d = dram_in("mask_attn", [128, 128])
    mskt_d = dram_in("mask_tril", [128, 128])
    eps5_d = dram_in("eps5", [128, 1])
    eps12_d = dram_in("eps12", [128, 1])
    qkb_d = dram_in("qkb", [128, 8])
    pvb_d = dram_in("pvb", [128, 8])
    bqc_d = dram_in("bqc", [128, 8])
    bkc_d = dram_in("bkc", [128, 8])
    b1r_d = dram_in("b1r", [128, 256])
    gwr_d = dram_in("gwr", [128, 64])
    grep_d = dram_in("grep", [128, D])
    brep_d = dram_in("brep", [128, D])
    qsr_d = dram_in("qsr", [128, D])
    ksr_d = dram_in("ksr", [128, D])
    ones_d = dram_in("onesr", [1, 128])
    bvr_d = dram_in("bvr", [1, D])
    opbr_d = dram_in("opbr", [1, D])
    gbr_d = dram_in("gbr", [1, 8])

    out_d = nc.dram_tensor("out", [S, D], F32, kind="ExternalOutput")
    A_d = nc.dram_tensor("A_sp", [S, D], F32,
                         kind="ExternalOutput" if "A" in debug else "Internal")
    hTsp_d = nc.dram_tensor("hT_sp", [128, NP * S], BF16,
                            kind="ExternalOutput" if "hT" in debug else "Internal")
    h2Tsp_d = nc.dram_tensor("h2T_sp", [128, NP * S], BF16,
                             kind="ExternalOutput" if "h2T" in debug else "Internal")
    dbg = {}
    for name, shape, dt in (("xnT", [128, NP * S], BF16),
                            ("qTf", [128, NT * D], BF16),
                            ("kTf", [128, NT * D], BF16),
                            ("vf", [128, NT * 1040], BF16),
                            ("osb", [128, NT * D], BF16),
                            ("gate", [128, 128], F32),
                            ("stats", [128, 32], F32),
                            ("moe", [128, 8 * D], F32)):
        if name in debug:
            dbg[name] = nc.dram_tensor("d_" + name, shape, dt, kind="ExternalOutput")

    SB = {}
    regions = []

    SBASE = 19200

    def sb(name, shape, dtype, off, live, overlay=False):
        off = off + SBASE
        nbytes = shape[1] * (4 if dtype == F32 else 2)
        if not overlay:
            for (n2, o2, nb2, l2) in regions:
                if not (off >= o2 + nb2 or o2 >= off + nbytes
                        or live[1] < l2[0] or l2[1] < live[0]):
                    raise RuntimeError(f"SBUF overlap {name} vs {n2}")
            regions.append((name, off, nbytes, live))
        assert off % 32 == 0, f"{name} offset not 32-aligned"
        assert off + nbytes <= 229376, f"{name} beyond SBUF"
        SB[name] = nc.alloc_sbuf_tensor_at(name, list(shape), dtype, offset=off)
        return SB[name]

    PERS = (0, 8)
    sb("ident_bf", [128, 128], BF16, 0, PERS)
    sb("mask_attn", [128, 128], BF16, 256, PERS)
    sb("mask_tril", [128, 128], BF16, 512, PERS)
    sb("eps5", [128, 1], F32, 768, PERS)
    sb("eps12", [128, 1], F32, 800, PERS)
    sb("qkb", [128, 8], F32, 832, PERS)
    sb("pvb", [128, 8], F32, 864, PERS)
    sb("bqc", [128, 8], F32, 896, PERS)
    sb("bkc", [128, 8], F32, 928, PERS)
    sb("b1sb", [128, 256], F32, 960, PERS)
    sb("gw", [128, 64], BF16, 1984, PERS)
    sb("gate", [128, 128], F32, 2112, PERS)
    sb("stats", [128, 32], F32, 2624, PERS)
    sb("onesr", [1, 128], BF16, 2752, PERS)
    sb("bvr", [1, D], BF16, 3008, PERS)
    sb("opbr", [1, D], BF16, 5056, PERS)
    sb("gbr", [1, 8], BF16, 7104, PERS)
    sb("b2all", [8, D], BF16, 7136, PERS)
    sb("grep_f", [128, D], F32, 9184, PERS)
    sb("brep_f", [128, D], F32, 13280, PERS)
    sb("grep_b", [128, D], BF16, 17376, PERS)
    sb("brep_b", [128, D], BF16, 19424, PERS)
    sb("qsr", [128, D], BF16, 21472, PERS)
    sb("ksr", [128, D], BF16, 23520, PERS)
    sb("ssq16", [128, 16], F32, 25568, PERS)
    sb("rsq16", [128, 16], F32, 25632, PERS)
    sb("tmp16", [128, 16], F32, 25696, PERS)
    sb("gexp", [128, 8], F32, 25760, PERS)
    sb("ones_c", [128, 1], BF16, 25792, PERS)
    BA = 25824
    sb("xsb", [128, NT * D], F32, BA + 0, (0, 0))
    sb("xnT", [128, NP * S], BF16, BA + 65536, (0, 2))
    sb("hTbuf", [128, NP * S], BF16, BA + 98304, (0, 0))
    sb("xn_scr", [128, 2 * D], F32, BA + 131072, (0, 0))
    sb("xnbf_scr", [128, 2 * D], BF16, BA + 139264, (0, 0))
    sb("hbf_scr", [128, 2 * D], BF16, BA + 143360, (0, 0))
    sb("WT0", [128, NP * D], BF16, BA + 0, (1, 5))
    sb("WT1", [128, NP * D], BF16, BA + 16384, (1, 5))
    sb("qk_inT", [128, NP * S], BF16, BA + 32768, (1, 1))
    sb("qT_a", [128, 8 * S], BF16, BA + 98304, (1, 4))
    sb("kT_a", [128, 8 * S], BF16, BA + 131072, (1, 4))
    sb("v_inT", [128, NP * S], BF16, BA + 32768, (2, 3))
    sb("vsb_a", [128, NT * D], BF16, BA + 65536, (3, 4))
    sb("oT_a", [128, 8 * S], BF16, BA + 32768, (4, 5))
    sb("osb_a", [128, 2 * D], BF16, BA + 163840, (4, 4))
    sb("scrA", [128, 8 * 128], BF16, BA + 167936, (4, 4))
    sb("Astage", [128, 2 * D], F32, BA + 172032, (5, 5))
    sb("xbuf", [128, 2 * D], F32, BA + 65536, (5, 5))
    sb("hT2", [128, NP * S], BF16, BA + 0, (6, 6))
    sb("Wn2", [128, NP * D], BF16, BA + 32768, (6, 6))
    sb("qTf", [128, NT * D], BF16, BA + 49152, (6, 7))
    sb("kTf", [128, NT * D], BF16, BA + 81920, (6, 7))
    sb("vf", [128, NT * 1040], BF16, BA + 114688, (6, 7))
    sb("qraw", [128, 2 * D], BF16, BA + 147968, (6, 6))
    sb("sqscr", [128, 2 * D], BF16, BA + 152064, (6, 6))
    sb("qsc", [128, 2 * D], BF16, BA + 156160, (6, 6))
    sb("osb_f", [128, NT * D], BF16, BA + 0, (7, 7))
    sb("scrF", [128, 8 * 512], BF16, BA + 32768, (7, 7))
    SB["oTf"] = SB["kTf"]
    sb("Wn3", [128, NP * D], BF16, BA + 32768, (8, 8))
    sb("out2f", [128, 2 * D], F32, BA + 0, (8, 8))
    sb("h2scr", [128, 2 * D], BF16, BA + 8192, (8, 8))
    sb("h2Tbuf", [128, NP * S], BF16, BA + 49152, (8, 8))
    sb("gatebf", [128, 8], BF16, BA + 12288, (8, 8))
    sb("h2Ttc", [128, 2 * NP * 1024], BF16, BA + 0, (9, 9))
    sb("w1h", [128, 8 * 2048], BF16, BA + 32768, (9, 9))
    sb("w2h", [128, 16 * 1024], BF16, BA + 65536, (9, 9))
    sb("hidT", [128, 16 * 1024], BF16, BA + 98304, (9, 9))
    sb("moeA", [128, 8 * D], F32, BA + 131072, (9, 9))
    sb("xstg", [128, 2 * D], F32, BA + 163840, (9, 9))
    sb("ostg", [128, 2 * D], F32, BA + 172032, (9, 9))
    sb("gateT", [8, S], BF16, BA + 180224, (8, 9))

    with ExitStack() as ctx:
        PA = ctx.enter_context(nc.psum_tensor("PA", [128, 1024], F32))
        PB = ctx.enter_context(nc.psum_tensor("PB", [128, 1024], F32))
        PC = ctx.enter_context(nc.psum_tensor("PC", [128, 512], F32))
        PD = ctx.enter_context(nc.psum_tensor("PD", [128, 512], F32))
        T0 = ctx.enter_context(nc.psum_tensor("T0", [128, 1024], BF16))
        T1 = ctx.enter_context(nc.psum_tensor("T1", [128, 1024], BF16))

        class Sem:
            def __init__(self, h):
                self.h = h
                self.n = 0

        sG = Sem(ctx.enter_context(nc.semaphore("sG")))
        sS = Sem(ctx.enter_context(nc.semaphore("sS")))
        sPE = Sem(ctx.enter_context(nc.semaphore("sPE")))
        sACT = Sem(ctx.enter_context(nc.semaphore("sACT")))
        sDVE = Sem(ctx.enter_context(nc.semaphore("sDVE")))
        sPL = Sem(ctx.enter_context(nc.semaphore("sPL")))
        sOUT = Sem(ctx.enter_context(nc.semaphore("sOUT")))
        ESEM = {"pe": sPE, "act": sACT, "dve": sDVE, "pool": sPL}

        EM = {q: [] for q in ("pe", "act", "dve", "pool", "syn")}

        def em(q, fn):
            EM[q].append([fn, None])

        def inc(q, sem, k=1):
            assert EM[q][-1][1] is None, "double inc on one instruction"
            EM[q][-1][1] = (sem.h, k)
            sem.n += k

        WAITLOG = []

        def wait(q, sem, val=None):
            v = sem.n if val is None else val
            if v <= 0:
                return
            WAITLOG.append((q, sem, v))
            em(q, lambda e, s=sem.h, vv=v: e.wait_ge(s, vv))

        def dve_gap():
            em("dve", lambda e: e.drain())

        def eng_inc(q):
            if q in ("act", "pool", "dve"):
                em(q, lambda e: e.drain())
            inc(q, ESEM[q], 1)

        def mm(out, lhsT, rhs, start=True, stop=True, tp=None, skip=False):
            em("pe", lambda e, o=out, l=lhsT, r=rhs, st=start, sp=stop, t=tp, sk=skip:
               e.matmul(o, lhsT=l, rhs=r, start=st, stop=sp, tile_position=t,
                        skip_group_check=sk))

        def trans(out, in_):
            em("pe", lambda e, o=out, i=in_: e.transpose(o, i, SB["ident_bf"][:]))

        def act_flush_inc(src_ap):
            eng_inc("act")

        def act(out, in_, func, bias=None, scale=None, accum=None):
            kw = {}
            if bias is not None:
                kw["bias"] = bias
            if scale is not None:
                kw["scale"] = scale
            if accum is not None:
                kw["accum_out"] = accum
            em("act", lambda e, o=out, i=in_, f=func, k=dict(kw): e.activation(o, i, f, **k))

        def barrier():
            snap = {q: ESEM[q].n for q in ESEM}
            for q in ("pe", "act", "dve"):
                for q2 in ("pe", "act", "dve", "pool"):
                    if q2 != q and snap[q2] > 0:
                        wait(q, ESEM[q2], snap[q2])

        slots = [PA[:, 0:512], PA[:, 512:1024], PB[:, 0:512], PB[:, 512:1024],
                 PC[:, 0:512], PD[:, 0:512]]
        slot_free = [None] * 6
        slot_cur = [0]

        def slot_guard(q):
            for i2 in range(6):
                if slot_free[i2] is not None:
                    wait(q, *slot_free[i2])
                slot_free[i2] = None

        def get_slot():
            i = slot_cur[0]
            slot_cur[0] = (i + 1) % 6
            if slot_free[i] is not None:
                wait("pe", *slot_free[i])
            return i, slots[i]

        def slot_done(i, sem):
            slot_free[i] = (sem, sem.n)

        tb = [T0, T1]
        tb_free = [None, None]
        tb_cur = [0]

        def get_tbank():
            i = tb_cur[0]
            tb_cur[0] = (i + 1) % 2
            if tb_free[i] is not None:
                wait("pe", *tb_free[i])
            return i, tb[i]

        def tbank_done(i, sem):
            tb_free[i] = (sem, sem.n)

        def chunks(name, b):
            return SB[name].ap().rearrange("q (a b) -> q a b", b=b)

        def chunks3(name, n1, n2):
            return SB[name].ap().rearrange("q (a c b) -> q a c b", a=n1, c=n2)

        def pload(dst_ap, src_ap):
            em("pool", lambda e, d=dst_ap, s=src_ap: e.dma_start(out=d, in_=s))
            inc("pool", sG, 16)
            return sG.n

        def sload(dst_ap, src_ap):
            em("syn", lambda e, d=dst_ap, s=src_ap: e.dma_start(out=d, in_=s))
            inc("syn", sS, 16)
            return sS.n

        def nat_ap(dram, rows=None):
            a = dram.ap() if rows is None else dram.ap()[rows[0]:rows[1], :]
            return a.rearrange("(p q) f -> q p f", q=128)

        def sbuf_nat(name):
            return chunks(name, D)

        pload(SB["ident_bf"][:], ident_d.ap())
        pload(SB["mask_attn"][:], mska_d.ap())
        pload(SB["mask_tril"][:], mskt_d.ap())
        pload(SB["grep_b"][:], grep_d.ap())
        pload(SB["brep_b"][:], brep_d.ap())
        pload(SB["qsr"][:], qsr_d.ap())
        pload(SB["ksr"][:], ksr_d.ap())
        pload(SB["onesr"][:], ones_d.ap())
        pload(SB["bvr"][:], bvr_d.ap())
        pload(SB["opbr"][:], opbr_d.ap())
        pload(SB["gbr"][:], gbr_d.ap())
        pload(SB["gw"][:], gwr_d.ap())
        g_const = pload(SB["b2all"][:], b2_d.ap())
        sload(SB["eps5"][:], eps5_d.ap())
        sload(SB["eps12"][:], eps12_d.ap())
        sload(SB["qkb"][:], qkb_d.ap())
        sload(SB["pvb"][:], pvb_d.ap())
        sload(SB["bqc"][:], bqc_d.ap())
        sload(SB["bkc"][:], bkc_d.ap())
        sload(SB["b1sb"][:], b1r_d.ap())
        sload(SB["grep_f"][:], grep_d.ap())
        s_consts = sload(SB["brep_f"][:], brep_d.ap())
        s_x = sload(chunks("xsb", D), x_d.ap().rearrange("(t q) f -> q t f", q=128))

        scr_free = {}
        trans_snap = {}

        def p0():
            INV = 1.0 / D
            st = SB["stats"]
            wait("dve", sS, s_x)
            wait("dve", sG, g_const)
            wait("act", sS, s_x)
            wait("pe", sG, g_const)
            P0SER = bool(os.environ.get("P0SER"))
            for t in range(int(os.environ.get("P0NT", NT))):
                cs = t % 2
                b = cs * 16
                xt = SB["xsb"][:, t * D:(t + 1) * D]
                if P0SER:
                    wait("dve", sACT)
                    wait("dve", sPE)
                    wait("act", sDVE)
                    wait("act", sPE)
                    wait("pe", sDVE)
                    wait("pe", sACT)
                if scr_free.get(("p0", cs)):
                    wait("act", *scr_free[("p0", cs)])
                em("dve", lambda e, o=st[:, b:b + 1], i=xt: e.reduce_sum(o, i, axis=AX.X))
                eng_inc("dve")
                em("dve", lambda e, o=SB["xn_scr"][:, cs * D:(cs + 1) * D], i=xt:
                   e.tensor_tensor(o, i, i, ALU.mult))
                em("dve", lambda e, o=st[:, b + 1:b + 2],
                   i=SB["xn_scr"][:, cs * D:(cs + 1) * D]: e.reduce_sum(o, i, axis=AX.X))
                em("dve", lambda e, o=st[:, b + 2:b + 3], i=st[:, b:b + 1]:
                   e.tensor_scalar(o, i, INV, None, ALU.mult))
                em("dve", lambda e, o=st[:, b + 3:b + 4], i=st[:, b + 1:b + 2]:
                   e.tensor_scalar(o, i, INV, None, ALU.mult))
                em("dve", lambda e, o=st[:, b + 4:b + 5], i=st[:, b + 2:b + 3]:
                   e.tensor_tensor(o, i, i, ALU.mult))
                dve_gap()
                em("dve", lambda e, o=st[:, b + 5:b + 6], a=st[:, b + 3:b + 4],
                   c=st[:, b + 4:b + 5]: e.tensor_tensor(o, a, c, ALU.subtract))
                eng_inc("dve")
                wait("act", sDVE)
                act(st[:, b + 6:b + 7], st[:, b + 5:b + 6], AF.Ln, bias=SB["eps5"][:])
                act(st[:, b + 7:b + 8], st[:, b + 6:b + 7], AF.Exp, scale=-0.5)
                act_flush_inc(st[:, b + 7:b + 8])
                wait("dve", sACT)
                em("dve", lambda e, o=st[:, b + 8:b + 9], a=st[:, b + 5:b + 6],
                   c=st[:, b + 7:b + 8]: e.tensor_tensor(o, a, c, ALU.mult))
                dve_gap()
                em("dve", lambda e, o=st[:, b + 9:b + 10], a=st[:, b + 8:b + 9],
                   c=st[:, b + 7:b + 8]: e.tensor_tensor(o, a, c, ALU.mult))
                eng_inc("dve")
                wait("act", sDVE)
                act(st[:, b + 10:b + 11], st[:, b + 9:b + 10], AF.Ln, bias=SB["eps5"][:])
                act(st[:, b + 11:b + 12], st[:, b + 10:b + 11], AF.Exp, scale=-0.5)
                act_flush_inc(st[:, b + 11:b + 12])
                wait("dve", sACT)
                if trans_snap.get(("p0", cs)):
                    wait("dve", sPE, trans_snap[("p0", cs)])
                xns = SB["xn_scr"][:, cs * D:(cs + 1) * D]
                xnb = SB["xnbf_scr"][:, cs * D:(cs + 1) * D]
                hbs = SB["hbf_scr"][:, cs * D:(cs + 1) * D]
                em("dve", lambda e, o=xns, i=xt, m=st[:, b + 2:b + 3], r=st[:, b + 7:b + 8]:
                   e.tensor_scalar(o, i, m, r, ALU.subtract, ALU.mult))
                em("dve", lambda e, o=xnb, i=xns: e.tensor_copy(o, i))
                em("dve", lambda e, o=hbs, i=xns, r=st[:, b + 11:b + 12]:
                   e.tensor_scalar(o, i, r, None, ALU.mult))
                em("dve", lambda e, o=hbs: e.tensor_tensor(o, o, SB["grep_b"][:], ALU.mult))
                em("dve", lambda e, o=hbs: e.tensor_tensor(o, o, SB["brep_b"][:], ALU.add))
                eng_inc("dve")
                d_ready = sDVE.n
                for (src, dstname) in ((xnb, "xnT"), (hbs, "hTbuf")):
                    for g in range(2):
                        ti, tbank = get_tbank()
                        wait("pe", sDVE, d_ready)
                        for j in range(4):
                            trans(tbank[:, j * 128:(j + 1) * 128],
                                  src[:, (g * 4 + j) * 128:(g * 4 + j + 1) * 128])
                        eng_inc("pe")
                        wait("act", sPE)
                        dst = chunks3(dstname, NP, NT)[:, g * 4:(g + 1) * 4, t, :]
                        act(dst, tbank[:, 0:512].rearrange("q (a b) -> q a b", b=128),
                            AF.Copy)
                        eng_inc("act")
                        tbank_done(ti, sACT)
                trans_snap[("p0", cs)] = sPE.n
                scr_free[("p0", cs)] = (sDVE, sDVE.n)
            wait("syn", sACT)
            return sload(hTsp_d.ap(), SB["hTbuf"].ap())

        def proj_T(dstname, WTname, srcname, bias_cols):
            wait("pe", sACT)
            wait("pe", sDVE)
            for po in range(NP):
                for tg in range(4):
                    i, sl = get_slot()
                    for pk in range(NP):
                        mm(sl, SB[WTname][:, pk * D + po * 128:pk * D + (po + 1) * 128],
                           SB[srcname][:, pk * S + tg * 512:pk * S + (tg + 1) * 512],
                           start=(pk == 0), stop=(pk == NP - 1))
                    eng_inc("pe")
                    dstap = SB[dstname][:, po * S + tg * 512:po * S + (tg + 1) * 512]
                    if tg % 2 == 0:
                        wait("act", sPE)
                        act(dstap, sl, AF.Identity, bias=bias_cols[:, po:po + 1])
                        eng_inc("act")
                        slot_done(i, sACT)
                    else:
                        wait("dve", sPE)
                        em("dve", lambda e, o=dstap, ii=sl, b=bias_cols[:, po:po + 1]:
                           e.tensor_scalar(o, ii, b, None, ALU.add))
                        eng_inc("dve")
                        slot_done(i, sDVE)

        def transpose_weight(natname, WTname, ready_g):
            wait("pe", sG, ready_g)
            for c in range(NP):
                for rg in range(2):
                    ti, tbank = get_tbank()
                    for j in range(4):
                        r = rg * 4 + j
                        trans(tbank[:, j * 128:(j + 1) * 128],
                              SB[natname][:, r * D + c * 128:r * D + (c + 1) * 128])
                    eng_inc("pe")
                    if (c + rg) % 2 == 0:
                        wait("act", sPE)
                        act(SB[WTname][:, c * D + rg * 512:c * D + (rg + 1) * 512],
                            tbank[:, 0:512], AF.Copy)
                        eng_inc("act")
                        tbank_done(ti, sACT)
                    else:
                        wait("dve", sPE)
                        em("dve", lambda e,
                           o=SB[WTname][:, c * D + rg * 512:c * D + (rg + 1) * 512],
                           i=tbank[:, 0:512]: e.tensor_copy(o, i))
                        eng_inc("dve")
                        tbank_done(ti, sDVE)
            return sPE.n

        def p1(s_hT):
            wait("act", sS, s_consts)
            wait("act", sS, s_hT)
            wait("pool", sDVE)
            wait("pool", sACT)
            em("pool", lambda e: e.memset(SB["ones_c"][:], 1.0))
            eng_inc("pool")
            pl_oc = sPL.n
            g1 = pload(sbuf_nat("WT0"), nat_ap(pqk_d))
            w0free = transpose_weight("WT0", "WT1", g1)
            proj_T("qk_inT", "WT1", "xnT", SB["qkb"])
            wait("pool", sPE, w0free)
            g2 = pload(sbuf_nat("WT0"), nat_ap(ipw_d, (0, D)))
            w0free = transpose_weight("WT0", "WT1", g2)
            proj_T("qT_a", "WT1", "qk_inT", SB["bqc"])
            wait("pool", sPE, w0free)
            g3 = pload(sbuf_nat("WT0"), nat_ap(ipw_d, (D, 2 * D)))
            w0free = transpose_weight("WT0", "WT1", g3)
            proj_T("kT_a", "WT1", "qk_inT", SB["bkc"])
            wait("pool", sPE, w0free)
            g4 = pload(sbuf_nat("WT0"), nat_ap(pv_d))
            w0free = transpose_weight("WT0", "WT1", g4)
            proj_T("v_inT", "WT1", "xnT", SB["pvb"])
            wait("pool", sPE, w0free)
            g5 = pload(sbuf_nat("WT0"), nat_ap(ipw_d, (2 * D, 3 * D)))
            w0free = transpose_weight("WT0", "WT1", g5)
            wait("pe", sACT)
            for t in range(NT):
                for dg in range(2):
                    i, sl = get_slot()
                    for pk in range(NP):
                        mm(sl, SB["v_inT"][:, pk * S + t * 128:pk * S + (t + 1) * 128],
                           SB["WT1"][:, pk * D + dg * 512:pk * D + (dg + 1) * 512],
                           start=(pk == 0), stop=False)
                    mm(sl, SB["onesr"][0:1, 0:128], SB["bvr"][0:1, dg * 512:(dg + 1) * 512],
                       start=False, stop=True)
                    eng_inc("pe")
                    wait("act", sPE)
                    act(SB["vsb_a"][:, t * D + dg * 512:t * D + (dg + 1) * 512], sl, AF.Copy)
                    eng_inc("act")
                    slot_done(i, sACT)
            wait("pe", sPL, pl_oc)
            wait("pe", sACT)
            slot_guard("pe")
            SCALE_A = float(1.0 / np.sqrt(np.float32(128)))
            sc_slots = [PC[:, 0:512], PD[:, 0:512]]
            sc_free_a = None
            o_free_a = None
            exp_snap = {}

            def emit_scores_qt(qt):
                nonlocal sc_free_a
                if sc_free_a is not None:
                    wait("pe", *sc_free_a)
                if scr_free.get(("osb", qt % 2)):
                    wait("pe", *scr_free[("osb", qt % 2)])
                for h in range(8):
                    mm(sc_slots[h // 4][:, (h % 4) * 128:(h % 4 + 1) * 128],
                       SB["kT_a"][:, h * S + qt * 128:h * S + (qt + 1) * 128],
                       SB["qT_a"][:, h * S + qt * 128:h * S + (qt + 1) * 128])
                eng_inc("pe")
                return sPE.n

            exp_snap[0] = emit_scores_qt(0)
            for qt in range(NT):
                cso = qt % 2
                osb = SB["osb_a"][:, cso * D:(cso + 1) * D]
                pe_sc = exp_snap.pop(qt)
                wait("act", sPE, pe_sc)
                if scr_free.get("scrA_p1d"):
                    wait("act", *scr_free["scrA_p1d"])
                for g2 in range(2):
                    act(SB["scrA"][:, g2 * 512:(g2 + 1) * 512], sc_slots[g2], AF.Exp,
                        scale=SCALE_A)
                eng_inc("act")
                a_exp = sACT.n
                wait("dve", sACT, a_exp)
                for h in range(8):
                    em("dve", lambda e, o=SB["scrA"][:, h * 128:(h + 1) * 128]:
                       e.tensor_tensor(o, o, SB["mask_attn"][:], ALU.mult))
                eng_inc("dve")
                d_mask = sDVE.n
                sc_free_a = (sACT, a_exp)
                if qt + 1 < NT:
                    exp_snap[qt + 1] = emit_scores_qt(qt + 1)
                if o_free_a is not None:
                    wait("pe", *o_free_a)
                wait("pe", sDVE, d_mask)
                for h in range(8):
                    mm(PA[:, h * 128:(h + 1) * 128],
                       SB["scrA"][:, h * 128:(h + 1) * 128],
                       SB["vsb_a"][:, qt * D + h * 128:qt * D + (h + 1) * 128])
                    mm(PB[:, h:h + 1], SB["scrA"][:, h * 128:(h + 1) * 128],
                       SB["ones_c"][:])
                eng_inc("pe")
                pe_o = sPE.n
                scr_free["scrA_p1d"] = (sPE, pe_o)
                wait("dve", sPE, pe_o)
                for h in range(8):
                    em("dve", lambda e, o=SB["stats"][:, 20 + h:21 + h],
                       i=PB[:, h:h + 1]: e.reciprocal(o, i))
                dve_gap()
                for h in range(8):
                    em("dve", lambda e, o=osb[:, h * 128:(h + 1) * 128],
                       i=PA[:, h * 128:(h + 1) * 128], r=SB["stats"][:, 20 + h:21 + h]:
                       e.tensor_scalar(o, i, r, None, ALU.mult))
                eng_inc("dve")
                o_free_a = (sDVE, sDVE.n)
                d_os = sDVE.n
                for hg in range(2):
                    ti, tbank = get_tbank()
                    wait("pe", sDVE, d_os)
                    for j in range(4):
                        h2 = hg * 4 + j
                        trans(tbank[:, j * 128:(j + 1) * 128],
                              osb[:, h2 * 128:(h2 + 1) * 128])
                    eng_inc("pe")
                    wait("act", sPE)
                    for j in range(4):
                        h2 = hg * 4 + j
                        act(SB["oT_a"][:, h2 * S + qt * 128:h2 * S + (qt + 1) * 128],
                            tbank[:, j * 128:(j + 1) * 128], AF.Copy)
                    eng_inc("act")
                    tbank_done(ti, sACT)
                scr_free[("osb", cso)] = (sPE, sPE.n)
            wait("pool", sPE, w0free)
            g6 = pload(sbuf_nat("WT0"), nat_ap(opw_d))
            transpose_weight("WT0", "WT1", g6)
            wait("pe", sACT)
            wait("syn", sPE)
            wait("syn", sDVE)
            x_re = x_d.ap().rearrange("(t q) f -> q t f", q=128)
            for t in range(NT):
                csA = t % 2
                astg = SB["Astage"][:, csA * D:(csA + 1) * D]
                xb = SB["xbuf"][:, csA * D:(csA + 1) * D]
                if scr_free.get(("A", csA)):
                    wait("act", *scr_free[("A", csA)])
                    wait("syn", *scr_free[("A", csA)])
                    wait("dve", *scr_free[("A", csA)])
                s_xt = sload(xb, x_re[:, t, :])
                for dg in range(2):
                    i, sl = get_slot()
                    for pk in range(NP):
                        mm(sl, SB["oT_a"][:, pk * S + t * 128:pk * S + (t + 1) * 128],
                           SB["WT1"][:, pk * D + dg * 512:pk * D + (dg + 1) * 512],
                           start=(pk == 0), stop=False)
                    mm(sl, SB["onesr"][0:1, 0:128], SB["opbr"][0:1, dg * 512:(dg + 1) * 512],
                       start=False, stop=True)
                    eng_inc("pe")
                    wait("act", sPE)
                    act(astg[:, dg * 512:(dg + 1) * 512], sl, AF.Copy)
                    eng_inc("act")
                    slot_done(i, sACT)
                wait("dve", sACT)
                wait("dve", sS, s_xt)
                em("dve", lambda e, o=astg, i=xb: e.tensor_tensor(o, o, i, ALU.add))
                eng_inc("dve")
                wait("syn", sDVE)
                sload(A_d.ap()[t * 128:(t + 1) * 128, :], astg)
                scr_free[("A", csA)] = (sS, sS.n)

        def st30():
            return SB["stats"][:, 30:31]

        def p2a(s_hT):
            wait("syn", sS, s_hT)
            wait("syn", sPE)
            wait("syn", sDVE)
            s_hT2 = sload(SB["hT2"].ap(), hTsp_d.ap())

            def qkv_gen(w_dram, dstT, scale_rep, mode):
                if "Wn2_readers" in scr_free:
                    wait("pool", *scr_free["Wn2_readers"])
                else:
                    wait("pool", sPE)
                g = pload(sbuf_nat("Wn2"), nat_ap(w_dram))
                wait("pe", sG, g)
                wait("pe", sS, s_hT2)
                qk_state = {}
                last_mm_snap = [0]

                def emit_mms(t):
                    ii = []
                    for dg in range(2):
                        i, sl = get_slot()
                        ii.append((i, sl))
                        for pk in range(NP):
                            mm(sl, SB["hT2"][:, pk * S + t * 128:pk * S + (t + 1) * 128],
                               SB["Wn2"][:, pk * D + dg * 512:pk * D + (dg + 1) * 512],
                               start=(pk == 0), stop=(pk == NP - 1))
                        eng_inc("pe")
                    qk_state[t] = (ii, sPE.n)
                    last_mm_snap[0] = sPE.n

                emit_mms(0)
                for t in range(NT):
                    ii, pe_q = qk_state.pop(t)
                    cs = t % 2
                    qrw = SB["qraw"][:, cs * D:(cs + 1) * D]
                    sqs = SB["sqscr"][:, cs * D:(cs + 1) * D]
                    qsc = SB["qsc"][:, cs * D:(cs + 1) * D]
                    wait("act", sPE, pe_q)
                    if mode == "v":
                        for dg in range(2):
                            dst = chunks3("vf", NT, 16)[:, t, dg * 8:(dg + 1) * 8, 0:64]
                            act(dst, ii[dg][1].rearrange("q (a b) -> q a b", b=64), AF.Copy)
                        eng_inc("act")
                        for dg in range(2):
                            slot_done(ii[dg][0], sACT)
                        if t + 1 < NT:
                            emit_mms(t + 1)
                        continue
                    if t >= 2 and ("qraw", mode, cs) in scr_free:
                        wait("act", *scr_free[("qraw", mode, cs)])
                    for dg in range(2):
                        act(qrw[:, dg * 512:(dg + 1) * 512], ii[dg][1], AF.Copy)
                    eng_inc("act")
                    for dg in range(2):
                        slot_done(ii[dg][0], sACT)
                    wait("dve", sACT)
                    em("dve", lambda e, o=sqs, i=qrw: e.tensor_tensor(o, i, i, ALU.mult))
                    em("dve", lambda e, o=SB["ssq16"][:, 0:16],
                       i=sqs.rearrange("q (a b) -> q a b", b=64):
                       e.reduce_sum(o, i, axis=AX.X))
                    eng_inc("dve")
                    wait("act", sDVE)
                    act(SB["tmp16"][:], SB["ssq16"][:], AF.Ln, bias=SB["eps12"][:])
                    act(SB["rsq16"][:], SB["tmp16"][:], AF.Exp, scale=-0.5)
                    act_flush_inc(SB["rsq16"][:, 15:16])
                    wait("dve", sACT)
                    if t >= 2:
                        wait("dve", sPE, scr_free.get(("qsc", mode, cs), (sPE, 0))[1])
                    rsqB = SB["rsq16"].ap()[:, 0:16].rearrange(
                        "q (a b) -> q a b", b=1).broadcast_to([128, 16, 64])
                    em("dve", lambda e, o=qsc.rearrange("q (a b) -> q a b", b=64),
                       i=qrw.rearrange("q (a b) -> q a b", b=64), r=rsqB:
                       e.tensor_tensor(o, i, r, ALU.mult))
                    em("dve", lambda e, o=qsc, i=scale_rep[:]:
                       e.tensor_tensor(o, o, i, ALU.mult))
                    eng_inc("dve")
                    scr_free[("qraw", mode, cs)] = (sDVE, sDVE.n)
                    d_q = sDVE.n
                    if t + 1 < NT:
                        emit_mms(t + 1)
                    for g2 in range(2):
                        ti, tbank = get_tbank()
                        wait("pe", sDVE, d_q)
                        for j in range(4):
                            hp = g2 * 4 + j
                            trans(tbank[:, j * 128:(j + 1) * 128],
                                  qsc[:, hp * 128:(hp + 1) * 128])
                        eng_inc("pe")
                        wait("act", sPE)
                        dst = SB[dstT].ap().rearrange("q (p t f) -> q p t f", p=NP, t=NT)[
                            :, g2 * 4:(g2 + 1) * 4, t, :]
                        act(dst, tbank[:, 0:512].rearrange("q (a b) -> q a b", b=128),
                            AF.Copy)
                        eng_inc("act")
                        tbank_done(ti, sACT)
                    scr_free[("qsc", mode, cs)] = (sPE, sPE.n)
                scr_free["Wn2_readers"] = (sPE, last_mm_snap[0])

            qkv_gen(wq_d, "qTf", SB["qsr"], "q")
            qkv_gen(wk_d, "kTf", SB["ksr"], "k")
            for t in range(NT):
                em("pool", lambda e, o=chunks3("vf", NT, 16)[:, t, :, 64:65]:
                   e.memset(o, 1.0))
            eng_inc("pool")
            qkv_gen(wv_d, "vf", None, "v")
            return sPL.n

        def p2b(pl_vones):
            scr_state = [None] * 8
            slots = [PC[:, 0:512], PD[:, 0:512],
                     T0.ap().bitcast(F32), T1.ap().bitcast(F32)]
            sc_state = [None] * 4
            o_ps = [PA, PB]
            LOOK = 3
            wait("pe", sPL, pl_vones)
            wait("pe", sACT)
            slot_guard("pe")
            for h in range(16):
                rb = (h % 2) * 64
                hp = h // 2
                head_dve_guard = sDVE.n
                hbase = hp * S
                wins = []
                for kc in range(NT):
                    for wi in range((NT - kc + 3) // 4):
                        tw = kc + wi * 4
                        wins.append((kc, wi, tw, min(4, NT - tw)))

                def emit_scores(i):
                    kc, wi, tw, ntile = wins[i]
                    si = i % 4
                    if sc_state[si] is not None:
                        wait("pe", *sc_state[si])
                    rhs = SB["qTf"][rb:rb + 64, hbase + tw * 128:hbase + tw * 128 + ntile * 128]
                    mm(slots[si][:, 0:ntile * 128],
                       SB["kTf"][rb:rb + 64, hbase + kc * 128:hbase + (kc + 1) * 128],
                       rhs, tp=(rb, 0))
                    eng_inc("pe")
                    return sPE.n

                pe_snap = {}
                emitted = [0]

                def ensure_scores(upto):
                    while emitted[0] <= min(upto, len(wins) - 1):
                        pe_snap[emitted[0]] = emit_scores(emitted[0])
                        emitted[0] += 1

                ensure_scores(1)
                for i0 in range(0, len(wins), 2):
                    pair = [i0] + ([i0 + 1] if i0 + 1 < len(wins) else [])
                    need_pe = []
                    for i in pair:
                        kc, wi, tw, ntile = wins[i]
                        si = i % 4
                        ci = i % 8
                        wait("act", sPE, pe_snap[i])
                        if scr_state[ci] is not None:
                            wait("act", *scr_state[ci])
                        scr = SB["scrF"][:, ci * 512:ci * 512 + ntile * 128]
                        act(scr, slots[si][:, 0:ntile * 128], AF.Exp, scale=10.0)
                        eng_inc("act")
                        sc_state[si] = (sACT, sACT.n)
                        a_exp = sACT.n
                        d_mask = None
                        if wi == 0:
                            wait("dve", sACT, a_exp)
                            em("dve", lambda e, o=SB["scrF"][:, ci * 512:ci * 512 + 128]:
                               e.tensor_tensor(o, o, SB["mask_tril"][:], ALU.mult))
                            eng_inc("dve")
                            d_mask = sDVE.n
                        need_pe.append((i, a_exp, d_mask))
                    ensure_scores(i0 + 1 + LOOK)
                    for (i, a_exp, d_mask) in need_pe:
                        kc, wi, tw, ntile = wins[i]
                        ci = i % 8
                        if d_mask is not None:
                            wait("pe", sDVE, d_mask)
                        else:
                            wait("pe", sACT, a_exp)
                        if i == 0:
                            wait("pe", sDVE, head_dve_guard)
                        for j in range(ntile):
                            qt = tw + j
                            ops = o_ps[qt // 8]
                            col = (qt % 8) * 128
                            mm(ops[:, col:col + 65],
                               SB["scrF"][:, ci * 512 + j * 128:ci * 512 + (j + 1) * 128],
                               SB["vf"][:, kc * 1040 + h * 65:kc * 1040 + (h + 1) * 65],
                               start=(kc == 0 and qt % 4 == 0), stop=(kc == qt), skip=True)
                        eng_inc("pe")
                        scr_state[ci] = (sPE, sPE.n)
                pe_head = sPE.n
                wait("dve", sPE, pe_head)
                for qt in range(NT):
                    ops = o_ps[qt // 8]
                    col = (qt % 8) * 128
                    em("dve", lambda e, o=st30(), i=ops[:, col + 64:col + 65]:
                       e.reciprocal(o, i))
                    dve_gap()
                    em("dve", lambda e, o=SB["osb_f"][:, qt * D + h * 64:qt * D + (h + 1) * 64],
                       i=ops[:, col:col + 64], r=st30():
                       e.tensor_scalar(o, i, r, None, ALU.mult))
                    eng_inc("dve")
                if h % 2 == 1:
                    d_pair = sDVE.n
                    for g2 in range(4):
                        ti = g2 % 2
                        tbank = [T0, T1][ti]
                        if sc_state[2 + ti] is not None:
                            wait("pe", *sc_state[2 + ti])
                        wait("pe", sDVE, d_pair)
                        for j in range(4):
                            qt = g2 * 4 + j
                            trans(tbank[:, j * 128:(j + 1) * 128],
                                  SB["osb_f"][:, qt * D + hp * 128:qt * D + (hp + 1) * 128])
                        eng_inc("pe")
                        wait("dve", sPE)
                        em("dve", lambda e,
                           o=SB["oTf"][:, hp * S + g2 * 512:hp * S + (g2 + 1) * 512],
                           i=tbank[:, 0:512]: e.tensor_copy(o, i))
                        eng_inc("dve")
                        sc_state[2 + ti] = (sDVE, sDVE.n)
                tb_free[0] = sc_state[2]
                tb_free[1] = sc_state[3]
            scr_free["p2b_pe_end"] = (sPE, sPE.n)

        def p3():
            wait("pool", sPE, scr_free.get("p2b_pe_end", (sPE, sPE.n))[1])
            g = pload(sbuf_nat("Wn3"), nat_ap(wo_d))
            wait("pe", sG, g)
            wait("pe", sACT)
            INV = 1.0 / D
            st = SB["stats"]
            p3_state = {}

            def emit_mms(t):
                ii = []
                for dg in range(2):
                    i, sl = get_slot()
                    ii.append((i, sl))
                    for pk in range(NP):
                        mm(sl, SB["oTf"][:, pk * S + t * 128:pk * S + (t + 1) * 128],
                           SB["Wn3"][:, pk * D + dg * 512:pk * D + (dg + 1) * 512],
                           start=(pk == 0), stop=(pk == NP - 1))
                    eng_inc("pe")
                p3_state[t] = (ii, sPE.n)

            emit_mms(0)
            for t in range(NT):
                ii, pe_q = p3_state.pop(t)
                cs = t % 2
                o2f = SB["out2f"][:, cs * D:(cs + 1) * D]
                h2s = SB["h2scr"][:, cs * D:(cs + 1) * D]
                wait("act", sPE, pe_q)
                if t >= 2 and ("o2f", cs) in scr_free:
                    wait("act", *scr_free[("o2f", cs)])
                for dg in range(2):
                    act(o2f[:, dg * 512:(dg + 1) * 512], ii[dg][1], AF.Copy)
                eng_inc("act")
                for dg in range(2):
                    slot_done(ii[dg][0], sACT)
                wait("dve", sACT)
                em("dve", lambda e, o=st[:, 0:1], i=o2f: e.reduce_sum(o, i, axis=AX.X))
                eng_inc("dve")
                act(h2s, o2f, AF.Square)
                eng_inc("act")
                wait("dve", sACT)
                em("dve", lambda e, o=st[:, 1:2], i=h2s: e.reduce_sum(o, i, axis=AX.X))
                em("dve", lambda e, o=st[:, 2:3], i=st[:, 0:1]:
                   e.tensor_scalar(o, i, INV, None, ALU.mult))
                em("dve", lambda e, o=st[:, 3:4], i=st[:, 1:2]:
                   e.tensor_scalar(o, i, INV, None, ALU.mult))
                em("dve", lambda e, o=st[:, 4:5], i=st[:, 2:3]:
                   e.tensor_tensor(o, i, i, ALU.mult))
                dve_gap()
                em("dve", lambda e, o=st[:, 5:6], a=st[:, 3:4], c=st[:, 4:5]:
                   e.tensor_tensor(o, a, c, ALU.subtract))
                eng_inc("dve")
                wait("act", sDVE)
                act(st[:, 6:7], st[:, 5:6], AF.Ln, bias=SB["eps5"][:])
                act(st[:, 7:8], st[:, 6:7], AF.Exp, scale=-0.5)
                act_flush_inc(st[:, 7:8])
                wait("dve", sACT)
                if t >= 2 and ("h2s", cs) in scr_free:
                    wait("dve", *scr_free[("h2s", cs)])
                em("dve", lambda e, o=h2s, i=o2f, m=st[:, 2:3], r=st[:, 7:8]:
                   e.tensor_scalar(o, i, m, r, ALU.subtract, ALU.mult))
                em("dve", lambda e, o=h2s:
                   e.tensor_tensor(o, o, SB["grep_b"][:], ALU.mult))
                em("dve", lambda e, o=h2s:
                   e.tensor_tensor(o, o, SB["brep_b"][:], ALU.add))
                eng_inc("dve")
                scr_free[("o2f", cs)] = (sDVE, sDVE.n)
                d_h2 = sDVE.n
                if t + 1 < NT:
                    emit_mms(t + 1)
                for g2 in range(2):
                    ti, tbank = get_tbank()
                    wait("pe", sDVE, d_h2)
                    for j in range(4):
                        p = g2 * 4 + j
                        trans(tbank[:, j * 128:(j + 1) * 128],
                              h2s[:, p * 128:(p + 1) * 128])
                    eng_inc("pe")
                    wait("act", sPE)
                    dst = chunks3("h2Tbuf", NP, NT)[:, g2 * 4:(g2 + 1) * 4, t, :]
                    act(dst, tbank[:, 0:512].rearrange("q (a b) -> q a b", b=128), AF.Copy)
                    eng_inc("act")
                    tbank_done(ti, sACT)
                scr_free[("h2s", cs)] = (sPE, sPE.n)
            wait("pe", sACT)
            slot_guard("pe")
            for t in range(NT):
                for pk in range(NP):
                    mm(PC[:, t * 8:(t + 1) * 8],
                       SB["h2Tbuf"][:, pk * S + t * 128:pk * S + (t + 1) * 128],
                       SB["gw"][:, pk * 8:(pk + 1) * 8], start=(pk == 0), stop=False)
                mm(PC[:, t * 8:(t + 1) * 8], SB["onesr"][0:1, 0:128],
                   SB["gbr"][0:1, 0:8], start=False, stop=True, skip=True)
            eng_inc("pe")
            wait("dve", sPE)
            wait("dve", sACT)
            gx = SB["out2f"].ap()[:, 0:128]
            em("dve", lambda e, o=SB["ssq16"][:, 0:16],
               i=PC.ap()[:, 0:128].rearrange("q (a b) -> q a b", b=8):
               e.reduce_max(o, i, axis=AX.X))
            dve_gap()
            em("dve", lambda e, o=gx,
               i=PC.ap()[:, 0:128].rearrange("q (a b) -> q a b", b=8),
               m=SB["ssq16"].ap()[:, 0:16].rearrange("q (a b) -> q a b", b=1)
               .broadcast_to([128, 16, 8]):
               e.tensor_tensor(o.rearrange("q (a b) -> q a b", b=8), i, m, ALU.subtract))
            eng_inc("dve")
            wait("act", sDVE)
            act(gx, gx, AF.Exp)
            act_flush_inc(SB["out2f"][:, 0:1])
            wait("dve", sACT)
            em("dve", lambda e, o=SB["rsq16"][:, 0:16],
               i=gx.rearrange("q (a b) -> q a b", b=8):
               e.reduce_sum(o, i, axis=AX.X))
            dve_gap()
            em("dve", lambda e, o=SB["tmp16"][:, 0:16], i=SB["rsq16"][:, 0:16]:
               e.reciprocal(o, i))
            dve_gap()
            em("dve", lambda e, o=SB["gate"][:, 0:128], i=gx,
               m=SB["tmp16"].ap()[:, 0:16].rearrange("q (a b) -> q a b", b=1)
               .broadcast_to([128, 16, 8]):
               e.tensor_tensor(o.rearrange("q (a b) -> q a b", b=8),
                               i.rearrange("q (a b) -> q a b", b=8), m, ALU.mult))
            em("dve", lambda e, o=SB["h2scr"][:, 0:128], i=SB["gate"][:, 0:128]:
               e.tensor_copy(o, i))
            eng_inc("dve")
            for g2 in range(2):
                ti, tbank = get_tbank()
                wait("pe", sDVE)
                for j in range(8):
                    t = g2 * 8 + j
                    trans(tbank[0:8, j * 128:(j + 1) * 128],
                          SB["h2scr"][:, t * 8:(t + 1) * 8])
                eng_inc("pe")
                wait("act", sPE)
                act(SB["gateT"][0:8, g2 * 1024:(g2 + 1) * 1024], tbank[0:8, 0:1024],
                    AF.Copy)
                eng_inc("act")
                tbank_done(ti, sACT)
            wait("syn", sACT)
            wait("syn", sDVE)
            return sload(h2Tsp_d.ap(), SB["h2Tbuf"].ap())

        def p4(s_h2T):
            INV = 1.0 / D
            st = SB["stats"]
            a_re = A_d.ap().rearrange("(t q) f -> q t f", q=128)
            o_re = out_d.ap().rearrange("(t q) f -> q t f", q=128)
            w1_free = w2_free = None
            wait("pool", sPE)
            wait("pool", sDVE)
            wait("pool", sACT)
            wait("pool", sS, s_h2T)
            slot_guard("pe")
            ph1_free = [None, None]
            ph2_free = [None, None]
            ph1_ps = [PA, PB]
            ph2_pairs = [(PC[:, 0:512], PD[:, 0:512]), (PA[:, 0:512], PA[:, 512:1024])]
            wait("syn", sS, s_h2T)
            wait("syn", sPE)
            wait("syn", sDVE)
            wait("syn", sACT)
            def epilogue_tile(tc, tt):
                g_t = tc * 8 + tt
                cs = tt % 2
                if scr_free.get(("epi", cs)):
                    wait("syn", *scr_free[("epi", cs)])
                    wait("act", *scr_free[("epi", cs)])
                    wait("dve", *scr_free[("epi", cs)])
                s_xa = sload(SB["xstg"].ap()[:, cs * D:(cs + 1) * D], a_re[:, g_t, :])
                moet = SB["moeA"][:, tt * D:(tt + 1) * D]
                em("dve", lambda e, o=st[:, 12:13], i=moet:
                   e.reduce_sum(o, i, axis=AX.X))
                eng_inc("dve")
                wait("act", sDVE)
                act(SB["ostg"][:, cs * D:(cs + 1) * D], moet, AF.Square)
                eng_inc("act")
                wait("dve", sACT)
                em("dve", lambda e, o=st[:, 13:14],
                   i=SB["ostg"][:, cs * D:(cs + 1) * D]:
                   e.reduce_sum(o, i, axis=AX.X))
                em("dve", lambda e, o=st[:, 14:15], i=st[:, 12:13]:
                   e.tensor_scalar(o, i, INV, None, ALU.mult))
                em("dve", lambda e, o=st[:, 15:16], i=st[:, 13:14]:
                   e.tensor_scalar(o, i, INV, None, ALU.mult))
                em("dve", lambda e, o=st[:, 16:17], i=st[:, 14:15]:
                   e.tensor_tensor(o, i, i, ALU.mult))
                dve_gap()
                em("dve", lambda e, o=st[:, 17:18], a=st[:, 15:16], c=st[:, 16:17]:
                   e.tensor_tensor(o, a, c, ALU.subtract))
                eng_inc("dve")
                wait("act", sDVE)
                act(st[:, 18:19], st[:, 17:18], AF.Ln, bias=SB["eps5"][:])
                act(st[:, 19:20], st[:, 18:19], AF.Exp, scale=-0.5)
                act_flush_inc(st[:, 19:20])
                wait("dve", sACT)
                wait("dve", sS, s_xa)
                ost = SB["ostg"][:, cs * D:(cs + 1) * D]
                em("dve", lambda e, o=ost, i=moet, m=st[:, 14:15], r=st[:, 19:20]:
                   e.tensor_scalar(o, i, m, r, ALU.subtract, ALU.mult))
                em("dve", lambda e, o=ost:
                   e.tensor_tensor(o, o, SB["grep_f"][:], ALU.mult))
                em("dve", lambda e, o=ost:
                   e.tensor_tensor(o, o, SB["brep_f"][:], ALU.add))
                em("dve", lambda e, o=ost, i=SB["xstg"][:, cs * D:(cs + 1) * D]:
                   e.tensor_tensor(o, o, i, ALU.add))
                eng_inc("dve")
                wait("syn", sDVE)
                em("syn", lambda e, d=o_re[:, g_t, :],
                   s=SB["ostg"].ap()[:, cs * D:(cs + 1) * D]:
                   e.dma_start(out=d, in_=s))
                inc("syn", sOUT, 16)
                scr_free[("epi", cs)] = (sOUT, sOUT.n)

            s_h2 = []
            for tc in range(2):
                s_h2.append(sload(
                    SB["h2Ttc"].ap()[:, tc * 8192:(tc + 1) * 8192]
                    .rearrange("q (p f) -> q p f", f=1024),
                    h2Tsp_d.ap().rearrange("q (p f) -> q p f", f=S)
                    [:, :, tc * 1024:(tc + 1) * 1024]))
            for tc in range(2):
                h2base = tc * 8192
                wait("pe", sG, g_const)
                wait("pe", sACT)
                for tt in range(8):
                    g_t = tc * 8 + tt
                    if ph2_free[0] is not None:
                        wait("pe", *ph2_free[0])
                    pL, pR = ph2_pairs[0]
                    mm(pL, SB["gateT"][0:8, g_t * 128:(g_t + 1) * 128],
                       SB["b2all"][0:8, 0:512])
                    mm(pR, SB["gateT"][0:8, g_t * 128:(g_t + 1) * 128],
                       SB["b2all"][0:8, 512:1024])
                    eng_inc("pe")
                    wait("dve", sPE)
                    for gg, psrc in ((0, pL), (1, pR)):
                        em("dve", lambda e, o=SB["moeA"][:, tt * D + gg * 512:
                                                         tt * D + (gg + 1) * 512],
                           i=psrc: e.tensor_copy(o, i))
                    eng_inc("dve")
                    ph2_free[0] = (sDVE, sDVE.n)
                for e_i in range(8):
                    for half in range(2):
                        if w1_free is not None:
                            wait("pool", *w1_free)
                        g_w1 = pload(chunks("w1h", 2048),
                                     w1_d.ap()[e_i, :, half * 2048:(half + 1) * 2048]
                                     .rearrange("(c q) f -> q c f", q=128))
                        if w2_free is not None:
                            wait("pool", *w2_free)
                        for piece in range(2):
                            g_w2 = pload(
                                SB["w2h"].ap()[:, piece * 8192:(piece + 1) * 8192]
                                .rearrange("q (c f) -> q c f", f=D),
                                w2_d.ap()[e_i, half * 2048 + piece * 1024:
                                          half * 2048 + (piece + 1) * 1024, :]
                                .rearrange("(c q) f -> q c f", q=128))
                        wait("pe", sG, g_w1)
                        wait("pe", sS, s_h2[tc])
                        for c in range(16):
                            pi = c % 2
                            if ph1_free[pi] is not None:
                                wait("pe", *ph1_free[pi])
                            for tg in range(2):
                                for pk in range(NP):
                                    mm(ph1_ps[pi][:, tg * 512:(tg + 1) * 512],
                                       SB["w1h"][:, pk * 2048 + c * 128:
                                                 pk * 2048 + (c + 1) * 128],
                                       SB["h2Ttc"][:, h2base + pk * 1024 + tg * 512:
                                                   h2base + pk * 1024 + (tg + 1) * 512],
                                       start=(pk == 0), stop=(pk == NP - 1))
                            eng_inc("pe")
                            wait("act", sPE)
                            bcol = SB["b1sb"][:, e_i * 32 + half * 16 + c:
                                              e_i * 32 + half * 16 + c + 1]
                            act(SB["hidT"][:, c * 1024:(c + 1) * 1024], ph1_ps[pi][:],
                                AF.Gelu_apprx_tanh, bias=bcol)
                            eng_inc("act")
                            ph1_free[pi] = (sACT, sACT.n)
                        w1_free = (sPE, sPE.n)
                        wait("pe", sG, g_w2)
                        wait("pe", sACT)
                        for tt in range(8):
                            pi = tt % 2
                            if ph2_free[pi] is not None:
                                wait("pe", *ph2_free[pi])
                            pL, pR = ph2_pairs[pi]
                            for c in range(16):
                                lhs = SB["hidT"][:, c * 1024 + tt * 128:
                                                 c * 1024 + (tt + 1) * 128]
                                stp = (c == 15)
                                mm(pL, lhs, SB["w2h"][:, c * 1024:c * 1024 + 512],
                                   start=(c == 0), stop=stp)
                                mm(pR, lhs, SB["w2h"][:, c * 1024 + 512:(c + 1) * 1024],
                                   start=(c == 0), stop=stp)
                            eng_inc("pe")
                            wait("dve", sPE)
                            gcol = SB["gate"][:, (tc * 8 + tt) * 8 + e_i:
                                              (tc * 8 + tt) * 8 + e_i + 1]
                            for gg, psrc in ((0, pL), (1, pR)):
                                dst = SB["moeA"][:, tt * D + gg * 512:tt * D + (gg + 1) * 512]
                                em("dve", lambda e, o=dst, i=psrc, r=gcol:
                                   e.scalar_tensor_tensor(o, i, r, o, ALU.mult, ALU.add))
                            eng_inc("dve")
                            ph2_free[pi] = (sDVE, sDVE.n)
                            if e_i == 7 and half == 1:
                                epilogue_tile(tc, tt)
                        w2_free = (sPE, sPE.n)
                        ph1_free[0] = (sDVE, sDVE.n)


        s_hT = p0()
        pl_vones = None
        if stage >= 1:
            barrier()
            p1(s_hT)
        if stage >= 2:
            barrier()
            pl_vones = p2a(s_hT)
        if stage >= 3:
            barrier()
            p2b(pl_vones)
            barrier()
            if "osb" in dbg:
                wait("syn", sACT)
                wait("syn", sDVE)
                em("syn", lambda e, d=dbg["osb"].ap(), s=SB["osb_f"].ap():
                   e.dma_start(out=d, in_=s))
                inc("syn", sOUT, 16)
                dbg.pop("osb")
            s_h2T = p3()
        if stage >= 4:
            barrier()
            p4(s_h2T)

        def dstore(name, sbname):
            if name in dbg:
                wait("syn", sACT)
                wait("syn", sDVE)
                em("syn", lambda e, d=dbg[name].ap(), s=SB[sbname].ap():
                   e.dma_start(out=d, in_=s))
                inc("syn", sOUT, 16)

        dstore("xnT", "xnT")
        dstore("qTf", "qTf")
        dstore("kTf", "kTf")
        dstore("vf", "vf")
        dstore("osb", "osb_f")
        dstore("gate", "gate")
        dstore("stats", "stats")
        dstore("moe", "moeA")

        wait("syn", sOUT, sOUT.n)
        wait("syn", sS, sS.n)
        wait("syn", sG, sG.n)

        for (q, sem, v) in WAITLOG:
            assert v <= sem.n, f"wait on {q} for {v} but sem tops out at {sem.n}"

        with nc.Block() as block:
            def runner(q):
                def go(eng):
                    for fn, incs in EM[q]:
                        r = fn(eng)
                        if incs is not None:
                            r.then_inc(*incs)
                return go
            block.tensor(runner("pe"))
            block.scalar(runner("act"))
            block.vector(runner("dve"))
            block.gpsimd(runner("pool"))
            block.sync(runner("syn"))

    return nc


def _prep(inputs):
    f32 = np.float32
    asf = lambda a: np.ascontiguousarray(np.asarray(a, f32))
    kq = np.arange(128)
    mask_attn = ((kq[:, None] // 4 == kq[None, :] // 4) &
                 (kq[:, None] != kq[None, :])).astype(f32)
    mask_tril = (kq[:, None] <= kq[None, :]).astype(f32)
    ipb = asf(inputs["L0_attn_in_proj_b"])
    shared = dict(
        pqk_w=asf(inputs["L0_attn_proj_qk_w"]),
        pv_w=asf(inputs["L0_attn_proj_v_w"]),
        ipw=asf(inputs["L0_attn_in_proj_w"]),
        opw=asf(inputs["L0_attn_out_proj_w"]),
        Wq=asf(inputs["L0_ffn_Wq"]),
        Wk=asf(inputs["L0_ffn_Wk"]),
        Wv=asf(inputs["L0_ffn_Wv"]),
        Wo=asf(inputs["L0_ffn_Wo"]),
        w1=asf(inputs["L0_ffn_w1"]),
        w2=asf(inputs["L0_ffn_w2"]),
        b2=asf(inputs["L0_ffn_b2"]),
        ident=np.eye(128, dtype=f32),
        mask_attn=mask_attn,
        mask_tril=mask_tril,
        eps5=np.full((128, 1), 1e-5, f32),
        eps12=np.full((128, 1), 1e-12, f32),
        qkb=np.ascontiguousarray(asf(inputs["L0_attn_proj_qk_b"]).reshape(8, 128).T),
        pvb=np.ascontiguousarray(asf(inputs["L0_attn_proj_v_b"]).reshape(8, 128).T),
        bqc=np.ascontiguousarray(ipb[0:1024].reshape(8, 128).T),
        bkc=np.ascontiguousarray(ipb[1024:2048].reshape(8, 128).T),
        b1r=np.ascontiguousarray(asf(inputs["L0_ffn_b1"]).reshape(8, 32, 128)
                                 .transpose(2, 0, 1).reshape(128, 256)),
        gwr=np.ascontiguousarray(asf(inputs["L0_ffn_gate_w"]).reshape(8, 128, 8)
                                 .transpose(1, 0, 2).reshape(128, 64)),
        grep=np.ascontiguousarray(np.broadcast_to(asf(inputs["L0_ffn_ln_g"]),
                                                  (128, 1024))),
        brep=np.ascontiguousarray(np.broadcast_to(asf(inputs["L0_ffn_ln_b"]),
                                                  (128, 1024))),
        qsr=np.ascontiguousarray(np.broadcast_to(
            asf(inputs["L0_ffn_q_scale"]).reshape(1, 1024), (128, 1024))),
        ksr=np.ascontiguousarray(np.broadcast_to(
            asf(inputs["L0_ffn_k_scale"]).reshape(1, 1024), (128, 1024))),
        onesr=np.ones((1, 128), f32),
        bvr=np.ascontiguousarray(ipb[2048:3072].reshape(1, 1024)),
        opbr=np.ascontiguousarray(asf(inputs["L0_attn_out_proj_b"]).reshape(1, 1024)),
        gbr=np.ascontiguousarray(asf(inputs["L0_ffn_gate_b"]).reshape(1, 8)),
    )
    x = asf(inputs["L0_attn_x"])
    return shared, x


def kernel(**inputs):
    shared, x = _prep(inputs)
    nc = build(stage=4)
    in_maps = [dict(shared, x=np.ascontiguousarray(x[i])) for i in range(NCORE)]
    res = run_bass_kernel_spmd(nc, in_maps, core_ids=list(range(NCORE)))
    return np.stack([np.asarray(res.results[i]["out"], np.float32)
                     for i in range(NCORE)], 0)


# version 11
# speedup vs baseline: 1.0152x; 1.0152x over previous
import os
from contextlib import ExitStack

import numpy as np

import concourse.bass as bass
import concourse.mybir as mybir
from concourse.bass_utils import run_bass_kernel_spmd

F32 = mybir.dt.float32
BF16 = mybir.dt.bfloat16
AF = mybir.ActivationFunctionType
ALU = mybir.AluOpType
AX = mybir.AxisListType

S = 2048
D = 1024
NT = 16
NP = 8
NCORE = 8


def build(stage=4, debug=()):
    nc = bass.Bass("TRN2", target_bir_lowering=False)

    def dram_in(name, shape):
        return nc.dram_tensor(name, list(shape), F32, kind="ExternalInput")

    x_d = dram_in("x", [S, D])
    pqk_d = dram_in("pqk_w", [D, D])
    pv_d = dram_in("pv_w", [D, D])
    ipw_d = dram_in("ipw", [3 * D, D])
    opw_d = dram_in("opw", [D, D])
    wq_d = dram_in("Wq", [D, D])
    wk_d = dram_in("Wk", [D, D])
    wv_d = dram_in("Wv", [D, D])
    wo_d = dram_in("Wo", [D, D])
    w1_d = dram_in("w1", [8, D, 4096])
    w2_d = dram_in("w2", [8, 4096, D])
    b2_d = dram_in("b2", [8, D])
    ident_d = dram_in("ident", [128, 128])
    mska_d = dram_in("mask_attn", [128, 128])
    mskt_d = dram_in("mask_tril", [128, 128])
    eps5_d = dram_in("eps5", [128, 1])
    eps12_d = dram_in("eps12", [128, 1])
    qkb_d = dram_in("qkb", [128, 8])
    pvb_d = dram_in("pvb", [128, 8])
    bqc_d = dram_in("bqc", [128, 8])
    bkc_d = dram_in("bkc", [128, 8])
    b1r_d = dram_in("b1r", [128, 256])
    gwr_d = dram_in("gwr", [128, 64])
    grep_d = dram_in("grep", [128, D])
    brep_d = dram_in("brep", [128, D])
    qsr_d = dram_in("qsr", [128, D])
    ksr_d = dram_in("ksr", [128, D])
    ones_d = dram_in("onesr", [1, 128])
    bvr_d = dram_in("bvr", [1, D])
    opbr_d = dram_in("opbr", [1, D])
    gbr_d = dram_in("gbr", [1, 8])

    out_d = nc.dram_tensor("out", [S, D], F32, kind="ExternalOutput")
    A_d = nc.dram_tensor("A_sp", [S, D], F32,
                         kind="ExternalOutput" if "A" in debug else "Internal")
    hTsp_d = nc.dram_tensor("hT_sp", [128, NP * S], BF16,
                            kind="ExternalOutput" if "hT" in debug else "Internal")
    h2Tsp_d = nc.dram_tensor("h2T_sp", [128, NP * S], BF16,
                             kind="ExternalOutput" if "h2T" in debug else "Internal")
    dbg = {}
    for name, shape, dt in (("xnT", [128, NP * S], BF16),
                            ("qTf", [128, NT * D], BF16),
                            ("kTf", [128, NT * D], BF16),
                            ("vf", [128, NT * 1040], BF16),
                            ("osb", [128, NT * D], BF16),
                            ("gate", [128, 128], F32),
                            ("stats", [128, 32], F32),
                            ("moe", [128, 8 * D], F32)):
        if name in debug:
            dbg[name] = nc.dram_tensor("d_" + name, shape, dt, kind="ExternalOutput")

    SB = {}
    regions = []

    SBASE = 19200

    def sb(name, shape, dtype, off, live, overlay=False):
        off = off + SBASE
        nbytes = shape[1] * (4 if dtype == F32 else 2)
        if not overlay:
            for (n2, o2, nb2, l2) in regions:
                if not (off >= o2 + nb2 or o2 >= off + nbytes
                        or live[1] < l2[0] or l2[1] < live[0]):
                    raise RuntimeError(f"SBUF overlap {name} vs {n2}")
            regions.append((name, off, nbytes, live))
        assert off % 32 == 0, f"{name} offset not 32-aligned"
        assert off + nbytes <= 229376, f"{name} beyond SBUF"
        SB[name] = nc.alloc_sbuf_tensor_at(name, list(shape), dtype, offset=off)
        return SB[name]

    PERS = (0, 8)
    sb("ident_bf", [128, 128], BF16, 0, PERS)
    sb("mask_attn", [128, 128], BF16, 256, PERS)
    sb("mask_tril", [128, 128], BF16, 512, PERS)
    sb("eps5", [128, 1], F32, 768, PERS)
    sb("eps12", [128, 1], F32, 800, PERS)
    sb("qkb", [128, 8], F32, 832, PERS)
    sb("pvb", [128, 8], F32, 864, PERS)
    sb("bqc", [128, 8], F32, 896, PERS)
    sb("bkc", [128, 8], F32, 928, PERS)
    sb("b1sb", [128, 256], F32, 960, PERS)
    sb("gw", [128, 64], BF16, 1984, PERS)
    sb("gate", [128, 128], F32, 2112, PERS)
    sb("stats", [128, 32], F32, 2624, PERS)
    sb("onesr", [1, 128], BF16, 2752, PERS)
    sb("bvr", [1, D], BF16, 3008, PERS)
    sb("opbr", [1, D], BF16, 5056, PERS)
    sb("gbr", [1, 8], BF16, 7104, PERS)
    sb("b2all", [8, D], BF16, 7136, PERS)
    sb("grep_f", [128, D], F32, 9184, PERS)
    sb("brep_f", [128, D], F32, 13280, PERS)
    sb("grep_b", [128, D], BF16, 17376, PERS)
    sb("brep_b", [128, D], BF16, 19424, PERS)
    sb("qsr", [128, D], BF16, 21472, PERS)
    sb("ksr", [128, D], BF16, 23520, PERS)
    sb("ssq16", [128, 16], F32, 25568, PERS)
    sb("rsq16", [128, 16], F32, 25632, PERS)
    sb("tmp16", [128, 16], F32, 25696, PERS)
    sb("gexp", [128, 8], F32, 25760, PERS)
    sb("ones_c", [128, 1], BF16, 25792, PERS)
    BA = 25824
    sb("xsb", [128, NT * D], F32, BA + 0, (0, 0))
    sb("xnT", [128, NP * S], BF16, BA + 65536, (0, 2))
    sb("hTbuf", [128, NP * S], BF16, BA + 98304, (0, 0))
    sb("xn_scr", [128, 2 * D], F32, BA + 131072, (0, 0))
    sb("xnbf_scr", [128, 2 * D], BF16, BA + 139264, (0, 0))
    sb("hbf_scr", [128, 2 * D], BF16, BA + 143360, (0, 0))
    sb("WT0", [128, NP * D], BF16, BA + 0, (1, 5))
    sb("WT1", [128, NP * D], BF16, BA + 16384, (1, 5))
    sb("qk_inT", [128, NP * S], BF16, BA + 32768, (1, 1))
    sb("qT_a", [128, 8 * S], BF16, BA + 98304, (1, 4))
    sb("kT_a", [128, 8 * S], BF16, BA + 131072, (1, 4))
    sb("v_inT", [128, NP * S], BF16, BA + 32768, (2, 3))
    sb("vsb_a", [128, NT * D], BF16, BA + 65536, (3, 4))
    sb("oT_a", [128, 8 * S], BF16, BA + 32768, (4, 5))
    sb("osb_a", [128, 2 * D], BF16, BA + 163840, (4, 4))
    sb("scrA", [128, 8 * 128], BF16, BA + 167936, (4, 4))
    sb("Astage", [128, 2 * D], F32, BA + 172032, (5, 5))
    sb("xbuf", [128, 2 * D], F32, BA + 65536, (5, 5))
    sb("hT2", [128, NP * S], BF16, BA + 0, (6, 6))
    sb("Wn2", [128, NP * D], BF16, BA + 32768, (6, 6))
    sb("qTf", [128, NT * D], BF16, BA + 49152, (6, 7))
    sb("kTf", [128, NT * D], BF16, BA + 81920, (6, 7))
    sb("vf", [128, NT * 1040], BF16, BA + 114688, (6, 7))
    sb("qraw", [128, 2 * D], BF16, BA + 147968, (6, 6))
    sb("sqscr", [128, 2 * D], BF16, BA + 152064, (6, 6))
    sb("qsc", [128, 2 * D], BF16, BA + 156160, (6, 6))
    sb("osb_f", [128, NT * D], BF16, BA + 0, (7, 7))
    sb("scrF", [128, 4 * 512], BF16, BA + 32768, (7, 7))
    SB["oTf"] = SB["kTf"]
    sb("Wn3", [128, NP * D], BF16, BA + 32768, (8, 8))
    sb("out2f", [128, 2 * D], F32, BA + 0, (8, 8))
    sb("h2scr", [128, 2 * D], BF16, BA + 8192, (8, 8))
    sb("h2Tbuf", [128, NP * S], BF16, BA + 49152, (8, 8))
    sb("gatebf", [128, 8], BF16, BA + 12288, (8, 8))
    sb("h2Ttc", [128, 2 * NP * 1024], BF16, BA + 0, (9, 9))
    sb("w1h", [128, 8 * 2048], BF16, BA + 32768, (9, 9))
    sb("w2h", [128, 16 * 1024], BF16, BA + 65536, (9, 9))
    sb("hidT", [128, 16 * 1024], BF16, BA + 98304, (9, 9))
    sb("moeA", [128, 8 * D], F32, BA + 131072, (9, 9))
    sb("xstg", [128, 2 * D], F32, BA + 163840, (9, 9))
    sb("ostg", [128, 2 * D], F32, BA + 172032, (9, 9))
    sb("gateT", [8, S], BF16, BA + 180224, (8, 9))

    with ExitStack() as ctx:
        PA = ctx.enter_context(nc.psum_tensor("PA", [128, 1024], F32))
        PB = ctx.enter_context(nc.psum_tensor("PB", [128, 1024], F32))
        PC = ctx.enter_context(nc.psum_tensor("PC", [128, 512], F32))
        PD = ctx.enter_context(nc.psum_tensor("PD", [128, 512], F32))
        T0 = ctx.enter_context(nc.psum_tensor("T0", [128, 1024], BF16))
        T1 = ctx.enter_context(nc.psum_tensor("T1", [128, 1024], BF16))

        class Sem:
            def __init__(self, h):
                self.h = h
                self.n = 0

        sG = Sem(ctx.enter_context(nc.semaphore("sG")))
        sS = Sem(ctx.enter_context(nc.semaphore("sS")))
        sPE = Sem(ctx.enter_context(nc.semaphore("sPE")))
        sACT = Sem(ctx.enter_context(nc.semaphore("sACT")))
        sDVE = Sem(ctx.enter_context(nc.semaphore("sDVE")))
        sPL = Sem(ctx.enter_context(nc.semaphore("sPL")))
        sOUT = Sem(ctx.enter_context(nc.semaphore("sOUT")))
        ESEM = {"pe": sPE, "act": sACT, "dve": sDVE, "pool": sPL}

        EM = {q: [] for q in ("pe", "act", "dve", "pool", "syn")}

        def em(q, fn):
            EM[q].append([fn, None])

        def inc(q, sem, k=1):
            assert EM[q][-1][1] is None, "double inc on one instruction"
            EM[q][-1][1] = (sem.h, k)
            sem.n += k

        WAITLOG = []

        def wait(q, sem, val=None):
            v = sem.n if val is None else val
            if v <= 0:
                return
            WAITLOG.append((q, sem, v))
            em(q, lambda e, s=sem.h, vv=v: e.wait_ge(s, vv))

        def dve_gap():
            em("dve", lambda e: e.drain())

        def eng_inc(q):
            if q in ("act", "pool"):
                em(q, lambda e: e.drain())
            inc(q, ESEM[q], 1)

        def mm(out, lhsT, rhs, start=True, stop=True, tp=None, skip=False):
            em("pe", lambda e, o=out, l=lhsT, r=rhs, st=start, sp=stop, t=tp, sk=skip:
               e.matmul(o, lhsT=l, rhs=r, start=st, stop=sp, tile_position=t,
                        skip_group_check=sk))

        def trans(out, in_):
            em("pe", lambda e, o=out, i=in_: e.transpose(o, i, SB["ident_bf"][:]))

        def act_flush_inc(src_ap):
            eng_inc("act")

        def act(out, in_, func, bias=None, scale=None, accum=None):
            kw = {}
            if bias is not None:
                kw["bias"] = bias
            if scale is not None:
                kw["scale"] = scale
            if accum is not None:
                kw["accum_out"] = accum
            em("act", lambda e, o=out, i=in_, f=func, k=dict(kw): e.activation(o, i, f, **k))

        def barrier():
            snap = {q: ESEM[q].n for q in ESEM}
            for q in ("pe", "act", "dve"):
                for q2 in ("pe", "act", "dve", "pool"):
                    if q2 != q and snap[q2] > 0:
                        wait(q, ESEM[q2], snap[q2])

        slots = [PA[:, 0:512], PA[:, 512:1024], PB[:, 0:512], PB[:, 512:1024]]
        slot_free = [None] * 4
        slot_cur = [0]

        def get_slot():
            i = slot_cur[0]
            slot_cur[0] = (i + 1) % 4
            if slot_free[i] is not None:
                wait("pe", *slot_free[i])
            return i, slots[i]

        def slot_done(i, sem):
            slot_free[i] = (sem, sem.n)

        tb = [T0, T1]
        tb_free = [None, None]
        tb_cur = [0]

        def get_tbank():
            i = tb_cur[0]
            tb_cur[0] = (i + 1) % 2
            if tb_free[i] is not None:
                wait("pe", *tb_free[i])
            return i, tb[i]

        def tbank_done(i, sem):
            tb_free[i] = (sem, sem.n)

        def chunks(name, b):
            return SB[name].ap().rearrange("q (a b) -> q a b", b=b)

        def chunks3(name, n1, n2):
            return SB[name].ap().rearrange("q (a c b) -> q a c b", a=n1, c=n2)

        def pload(dst_ap, src_ap):
            em("pool", lambda e, d=dst_ap, s=src_ap: e.dma_start(out=d, in_=s))
            inc("pool", sG, 16)
            return sG.n

        def sload(dst_ap, src_ap):
            em("syn", lambda e, d=dst_ap, s=src_ap: e.dma_start(out=d, in_=s))
            inc("syn", sS, 16)
            return sS.n

        def nat_ap(dram, rows=None):
            a = dram.ap() if rows is None else dram.ap()[rows[0]:rows[1], :]
            return a.rearrange("(p q) f -> q p f", q=128)

        def sbuf_nat(name):
            return chunks(name, D)

        pload(SB["ident_bf"][:], ident_d.ap())
        pload(SB["mask_attn"][:], mska_d.ap())
        pload(SB["mask_tril"][:], mskt_d.ap())
        pload(SB["grep_b"][:], grep_d.ap())
        pload(SB["brep_b"][:], brep_d.ap())
        pload(SB["qsr"][:], qsr_d.ap())
        pload(SB["ksr"][:], ksr_d.ap())
        pload(SB["onesr"][:], ones_d.ap())
        pload(SB["bvr"][:], bvr_d.ap())
        pload(SB["opbr"][:], opbr_d.ap())
        pload(SB["gbr"][:], gbr_d.ap())
        pload(SB["gw"][:], gwr_d.ap())
        g_const = pload(SB["b2all"][:], b2_d.ap())
        sload(SB["eps5"][:], eps5_d.ap())
        sload(SB["eps12"][:], eps12_d.ap())
        sload(SB["qkb"][:], qkb_d.ap())
        sload(SB["pvb"][:], pvb_d.ap())
        sload(SB["bqc"][:], bqc_d.ap())
        sload(SB["bkc"][:], bkc_d.ap())
        sload(SB["b1sb"][:], b1r_d.ap())
        sload(SB["grep_f"][:], grep_d.ap())
        s_consts = sload(SB["brep_f"][:], brep_d.ap())
        s_x = sload(chunks("xsb", D), x_d.ap().rearrange("(t q) f -> q t f", q=128))

        scr_free = {}
        trans_snap = {}

        def p0():
            INV = 1.0 / D
            st = SB["stats"]
            wait("dve", sS, s_x)
            wait("dve", sG, g_const)
            wait("act", sS, s_x)
            wait("pe", sG, g_const)
            P0SER = bool(os.environ.get("P0SER"))
            for t in range(int(os.environ.get("P0NT", NT))):
                cs = t % 2
                b = cs * 16
                xt = SB["xsb"][:, t * D:(t + 1) * D]
                if P0SER:
                    wait("dve", sACT)
                    wait("dve", sPE)
                    wait("act", sDVE)
                    wait("act", sPE)
                    wait("pe", sDVE)
                    wait("pe", sACT)
                if scr_free.get(("p0", cs)):
                    wait("act", *scr_free[("p0", cs)])
                em("dve", lambda e, o=st[:, b:b + 1], i=xt: e.reduce_sum(o, i, axis=AX.X))
                eng_inc("dve")
                em("dve", lambda e, o=SB["xn_scr"][:, cs * D:(cs + 1) * D], i=xt:
                   e.tensor_tensor(o, i, i, ALU.mult))
                em("dve", lambda e, o=st[:, b + 1:b + 2],
                   i=SB["xn_scr"][:, cs * D:(cs + 1) * D]: e.reduce_sum(o, i, axis=AX.X))
                em("dve", lambda e, o=st[:, b + 2:b + 3], i=st[:, b:b + 1]:
                   e.tensor_scalar(o, i, INV, None, ALU.mult))
                em("dve", lambda e, o=st[:, b + 3:b + 4], i=st[:, b + 1:b + 2]:
                   e.tensor_scalar(o, i, INV, None, ALU.mult))
                em("dve", lambda e, o=st[:, b + 4:b + 5], i=st[:, b + 2:b + 3]:
                   e.tensor_tensor(o, i, i, ALU.mult))
                dve_gap()
                em("dve", lambda e, o=st[:, b + 5:b + 6], a=st[:, b + 3:b + 4],
                   c=st[:, b + 4:b + 5]: e.tensor_tensor(o, a, c, ALU.subtract))
                eng_inc("dve")
                wait("act", sDVE)
                act(st[:, b + 6:b + 7], st[:, b + 5:b + 6], AF.Ln, bias=SB["eps5"][:])
                act(st[:, b + 7:b + 8], st[:, b + 6:b + 7], AF.Exp, scale=-0.5)
                act_flush_inc(st[:, b + 7:b + 8])
                wait("dve", sACT)
                em("dve", lambda e, o=st[:, b + 8:b + 9], a=st[:, b + 5:b + 6],
                   c=st[:, b + 7:b + 8]: e.tensor_tensor(o, a, c, ALU.mult))
                dve_gap()
                em("dve", lambda e, o=st[:, b + 9:b + 10], a=st[:, b + 8:b + 9],
                   c=st[:, b + 7:b + 8]: e.tensor_tensor(o, a, c, ALU.mult))
                eng_inc("dve")
                wait("act", sDVE)
                act(st[:, b + 10:b + 11], st[:, b + 9:b + 10], AF.Ln, bias=SB["eps5"][:])
                act(st[:, b + 11:b + 12], st[:, b + 10:b + 11], AF.Exp, scale=-0.5)
                act_flush_inc(st[:, b + 11:b + 12])
                wait("dve", sACT)
                if trans_snap.get(("p0", cs)):
                    wait("dve", sPE, trans_snap[("p0", cs)])
                xns = SB["xn_scr"][:, cs * D:(cs + 1) * D]
                xnb = SB["xnbf_scr"][:, cs * D:(cs + 1) * D]
                hbs = SB["hbf_scr"][:, cs * D:(cs + 1) * D]
                em("dve", lambda e, o=xns, i=xt, m=st[:, b + 2:b + 3], r=st[:, b + 7:b + 8]:
                   e.tensor_scalar(o, i, m, r, ALU.subtract, ALU.mult))
                em("dve", lambda e, o=xnb, i=xns: e.tensor_copy(o, i))
                em("dve", lambda e, o=hbs, i=xns, r=st[:, b + 11:b + 12]:
                   e.tensor_scalar(o, i, r, None, ALU.mult))
                em("dve", lambda e, o=hbs: e.tensor_tensor(o, o, SB["grep_b"][:], ALU.mult))
                em("dve", lambda e, o=hbs: e.tensor_tensor(o, o, SB["brep_b"][:], ALU.add))
                eng_inc("dve")
                d_ready = sDVE.n
                for (src, dstname) in ((xnb, "xnT"), (hbs, "hTbuf")):
                    for g in range(2):
                        ti, tbank = get_tbank()
                        wait("pe", sDVE, d_ready)
                        for j in range(4):
                            trans(tbank[:, j * 128:(j + 1) * 128],
                                  src[:, (g * 4 + j) * 128:(g * 4 + j + 1) * 128])
                        eng_inc("pe")
                        wait("act", sPE)
                        dst = chunks3(dstname, NP, NT)[:, g * 4:(g + 1) * 4, t, :]
                        act(dst, tbank[:, 0:512].rearrange("q (a b) -> q a b", b=128),
                            AF.Copy)
                        eng_inc("act")
                        tbank_done(ti, sACT)
                trans_snap[("p0", cs)] = sPE.n
                scr_free[("p0", cs)] = (sDVE, sDVE.n)
            wait("syn", sACT)
            return sload(hTsp_d.ap(), SB["hTbuf"].ap())

        def proj_T(dstname, WTname, srcname, bias_cols):
            wait("pe", sACT)
            wait("pe", sDVE)
            for po in range(NP):
                for tg in range(4):
                    i, sl = get_slot()
                    for pk in range(NP):
                        mm(sl, SB[WTname][:, pk * D + po * 128:pk * D + (po + 1) * 128],
                           SB[srcname][:, pk * S + tg * 512:pk * S + (tg + 1) * 512],
                           start=(pk == 0), stop=(pk == NP - 1))
                    eng_inc("pe")
                    dstap = SB[dstname][:, po * S + tg * 512:po * S + (tg + 1) * 512]
                    if tg % 2 == 0:
                        wait("act", sPE)
                        act(dstap, sl, AF.Identity, bias=bias_cols[:, po:po + 1])
                        eng_inc("act")
                        slot_done(i, sACT)
                    else:
                        wait("dve", sPE)
                        em("dve", lambda e, o=dstap, ii=sl, b=bias_cols[:, po:po + 1]:
                           e.tensor_scalar(o, ii, b, None, ALU.add))
                        eng_inc("dve")
                        slot_done(i, sDVE)

        def transpose_weight(natname, WTname, ready_g):
            wait("pe", sG, ready_g)
            for c in range(NP):
                for rg in range(2):
                    ti, tbank = get_tbank()
                    for j in range(4):
                        r = rg * 4 + j
                        trans(tbank[:, j * 128:(j + 1) * 128],
                              SB[natname][:, r * D + c * 128:r * D + (c + 1) * 128])
                    eng_inc("pe")
                    if (c + rg) % 2 == 0:
                        wait("act", sPE)
                        act(SB[WTname][:, c * D + rg * 512:c * D + (rg + 1) * 512],
                            tbank[:, 0:512], AF.Copy)
                        eng_inc("act")
                        tbank_done(ti, sACT)
                    else:
                        wait("dve", sPE)
                        em("dve", lambda e,
                           o=SB[WTname][:, c * D + rg * 512:c * D + (rg + 1) * 512],
                           i=tbank[:, 0:512]: e.tensor_copy(o, i))
                        eng_inc("dve")
                        tbank_done(ti, sDVE)
            return sPE.n

        def p1(s_hT):
            wait("act", sS, s_consts)
            wait("act", sS, s_hT)
            wait("pool", sDVE)
            wait("pool", sACT)
            em("pool", lambda e: e.memset(SB["ones_c"][:], 1.0))
            eng_inc("pool")
            pl_oc = sPL.n
            g1 = pload(sbuf_nat("WT0"), nat_ap(pqk_d))
            w0free = transpose_weight("WT0", "WT1", g1)
            proj_T("qk_inT", "WT1", "xnT", SB["qkb"])
            wait("pool", sPE, w0free)
            g2 = pload(sbuf_nat("WT0"), nat_ap(ipw_d, (0, D)))
            w0free = transpose_weight("WT0", "WT1", g2)
            proj_T("qT_a", "WT1", "qk_inT", SB["bqc"])
            wait("pool", sPE, w0free)
            g3 = pload(sbuf_nat("WT0"), nat_ap(ipw_d, (D, 2 * D)))
            w0free = transpose_weight("WT0", "WT1", g3)
            proj_T("kT_a", "WT1", "qk_inT", SB["bkc"])
            wait("pool", sPE, w0free)
            g4 = pload(sbuf_nat("WT0"), nat_ap(pv_d))
            w0free = transpose_weight("WT0", "WT1", g4)
            proj_T("v_inT", "WT1", "xnT", SB["pvb"])
            wait("pool", sPE, w0free)
            g5 = pload(sbuf_nat("WT0"), nat_ap(ipw_d, (2 * D, 3 * D)))
            w0free = transpose_weight("WT0", "WT1", g5)
            wait("pe", sACT)
            for t in range(NT):
                for dg in range(2):
                    i, sl = get_slot()
                    for pk in range(NP):
                        mm(sl, SB["v_inT"][:, pk * S + t * 128:pk * S + (t + 1) * 128],
                           SB["WT1"][:, pk * D + dg * 512:pk * D + (dg + 1) * 512],
                           start=(pk == 0), stop=False)
                    mm(sl, SB["onesr"][0:1, 0:128], SB["bvr"][0:1, dg * 512:(dg + 1) * 512],
                       start=False, stop=True)
                    eng_inc("pe")
                    wait("act", sPE)
                    act(SB["vsb_a"][:, t * D + dg * 512:t * D + (dg + 1) * 512], sl, AF.Copy)
                    eng_inc("act")
                    slot_done(i, sACT)
            wait("pe", sPL, pl_oc)
            wait("pe", sACT)
            SCALE_A = float(1.0 / np.sqrt(np.float32(128)))
            sc_slots = [PC[:, 0:512], PD[:, 0:512]]
            sc_free_a = None
            o_free_a = None
            exp_snap = {}

            def emit_scores_qt(qt):
                nonlocal sc_free_a
                if sc_free_a is not None:
                    wait("pe", *sc_free_a)
                if scr_free.get(("osb", qt % 2)):
                    wait("pe", *scr_free[("osb", qt % 2)])
                for h in range(8):
                    mm(sc_slots[h // 4][:, (h % 4) * 128:(h % 4 + 1) * 128],
                       SB["kT_a"][:, h * S + qt * 128:h * S + (qt + 1) * 128],
                       SB["qT_a"][:, h * S + qt * 128:h * S + (qt + 1) * 128])
                eng_inc("pe")
                return sPE.n

            exp_snap[0] = emit_scores_qt(0)
            for qt in range(NT):
                cso = qt % 2
                osb = SB["osb_a"][:, cso * D:(cso + 1) * D]
                pe_sc = exp_snap.pop(qt)
                wait("act", sPE, pe_sc)
                if scr_free.get("scrA_p1d"):
                    wait("act", *scr_free["scrA_p1d"])
                for g2 in range(2):
                    act(SB["scrA"][:, g2 * 512:(g2 + 1) * 512], sc_slots[g2], AF.Exp,
                        scale=SCALE_A)
                eng_inc("act")
                a_exp = sACT.n
                wait("dve", sACT, a_exp)
                for h in range(8):
                    em("dve", lambda e, o=SB["scrA"][:, h * 128:(h + 1) * 128]:
                       e.tensor_tensor(o, o, SB["mask_attn"][:], ALU.mult))
                eng_inc("dve")
                d_mask = sDVE.n
                sc_free_a = (sACT, a_exp)
                if qt + 1 < NT:
                    exp_snap[qt + 1] = emit_scores_qt(qt + 1)
                if o_free_a is not None:
                    wait("pe", *o_free_a)
                wait("pe", sDVE, d_mask)
                for h in range(8):
                    mm(PA[:, h * 128:(h + 1) * 128],
                       SB["scrA"][:, h * 128:(h + 1) * 128],
                       SB["vsb_a"][:, qt * D + h * 128:qt * D + (h + 1) * 128])
                    mm(PB[:, h:h + 1], SB["scrA"][:, h * 128:(h + 1) * 128],
                       SB["ones_c"][:])
                eng_inc("pe")
                pe_o = sPE.n
                scr_free["scrA_p1d"] = (sPE, pe_o)
                wait("dve", sPE, pe_o)
                for h in range(8):
                    em("dve", lambda e, o=SB["stats"][:, 20 + h:21 + h],
                       i=PB[:, h:h + 1]: e.reciprocal(o, i))
                dve_gap()
                for h in range(8):
                    em("dve", lambda e, o=osb[:, h * 128:(h + 1) * 128],
                       i=PA[:, h * 128:(h + 1) * 128], r=SB["stats"][:, 20 + h:21 + h]:
                       e.tensor_scalar(o, i, r, None, ALU.mult))
                eng_inc("dve")
                o_free_a = (sDVE, sDVE.n)
                d_os = sDVE.n
                for hg in range(2):
                    ti, tbank = get_tbank()
                    wait("pe", sDVE, d_os)
                    for j in range(4):
                        h2 = hg * 4 + j
                        trans(tbank[:, j * 128:(j + 1) * 128],
                              osb[:, h2 * 128:(h2 + 1) * 128])
                    eng_inc("pe")
                    wait("act", sPE)
                    for j in range(4):
                        h2 = hg * 4 + j
                        act(SB["oT_a"][:, h2 * S + qt * 128:h2 * S + (qt + 1) * 128],
                            tbank[:, j * 128:(j + 1) * 128], AF.Copy)
                    eng_inc("act")
                    tbank_done(ti, sACT)
                scr_free[("osb", cso)] = (sPE, sPE.n)
            wait("pool", sPE, w0free)
            g6 = pload(sbuf_nat("WT0"), nat_ap(opw_d))
            transpose_weight("WT0", "WT1", g6)
            wait("pe", sACT)
            wait("syn", sPE)
            wait("syn", sDVE)
            x_re = x_d.ap().rearrange("(t q) f -> q t f", q=128)
            for t in range(NT):
                csA = t % 2
                astg = SB["Astage"][:, csA * D:(csA + 1) * D]
                xb = SB["xbuf"][:, csA * D:(csA + 1) * D]
                if scr_free.get(("A", csA)):
                    wait("act", *scr_free[("A", csA)])
                    wait("syn", *scr_free[("A", csA)])
                    wait("dve", *scr_free[("A", csA)])
                s_xt = sload(xb, x_re[:, t, :])
                for dg in range(2):
                    i, sl = get_slot()
                    for pk in range(NP):
                        mm(sl, SB["oT_a"][:, pk * S + t * 128:pk * S + (t + 1) * 128],
                           SB["WT1"][:, pk * D + dg * 512:pk * D + (dg + 1) * 512],
                           start=(pk == 0), stop=False)
                    mm(sl, SB["onesr"][0:1, 0:128], SB["opbr"][0:1, dg * 512:(dg + 1) * 512],
                       start=False, stop=True)
                    eng_inc("pe")
                    wait("act", sPE)
                    act(astg[:, dg * 512:(dg + 1) * 512], sl, AF.Copy)
                    eng_inc("act")
                    slot_done(i, sACT)
                wait("dve", sACT)
                wait("dve", sS, s_xt)
                em("dve", lambda e, o=astg, i=xb: e.tensor_tensor(o, o, i, ALU.add))
                eng_inc("dve")
                wait("syn", sDVE)
                sload(A_d.ap()[t * 128:(t + 1) * 128, :], astg)
                scr_free[("A", csA)] = (sS, sS.n)

        def st30():
            return SB["stats"][:, 30:31]

        def p2a(s_hT):
            wait("syn", sS, s_hT)
            wait("syn", sPE)
            wait("syn", sDVE)
            s_hT2 = sload(SB["hT2"].ap(), hTsp_d.ap())

            def qkv_gen(w_dram, dstT, scale_rep, mode):
                if "Wn2_readers" in scr_free:
                    wait("pool", *scr_free["Wn2_readers"])
                else:
                    wait("pool", sPE)
                g = pload(sbuf_nat("Wn2"), nat_ap(w_dram))
                wait("pe", sG, g)
                wait("pe", sS, s_hT2)
                qk_state = {}
                last_mm_snap = [0]

                def emit_mms(t):
                    ii = []
                    for dg in range(2):
                        i, sl = get_slot()
                        ii.append((i, sl))
                        for pk in range(NP):
                            mm(sl, SB["hT2"][:, pk * S + t * 128:pk * S + (t + 1) * 128],
                               SB["Wn2"][:, pk * D + dg * 512:pk * D + (dg + 1) * 512],
                               start=(pk == 0), stop=(pk == NP - 1))
                        eng_inc("pe")
                    qk_state[t] = (ii, sPE.n)
                    last_mm_snap[0] = sPE.n

                emit_mms(0)
                for t in range(NT):
                    ii, pe_q = qk_state.pop(t)
                    cs = t % 2
                    qrw = SB["qraw"][:, cs * D:(cs + 1) * D]
                    sqs = SB["sqscr"][:, cs * D:(cs + 1) * D]
                    qsc = SB["qsc"][:, cs * D:(cs + 1) * D]
                    wait("act", sPE, pe_q)
                    if mode == "v":
                        for dg in range(2):
                            dst = chunks3("vf", NT, 16)[:, t, dg * 8:(dg + 1) * 8, 0:64]
                            act(dst, ii[dg][1].rearrange("q (a b) -> q a b", b=64), AF.Copy)
                        eng_inc("act")
                        for dg in range(2):
                            slot_done(ii[dg][0], sACT)
                        if t + 1 < NT:
                            emit_mms(t + 1)
                        continue
                    if t >= 2 and ("qraw", mode, cs) in scr_free:
                        wait("act", *scr_free[("qraw", mode, cs)])
                    for dg in range(2):
                        act(qrw[:, dg * 512:(dg + 1) * 512], ii[dg][1], AF.Copy)
                    eng_inc("act")
                    for dg in range(2):
                        slot_done(ii[dg][0], sACT)
                    wait("dve", sACT)
                    em("dve", lambda e, o=sqs, i=qrw: e.tensor_tensor(o, i, i, ALU.mult))
                    em("dve", lambda e, o=SB["ssq16"][:, 0:16],
                       i=sqs.rearrange("q (a b) -> q a b", b=64):
                       e.reduce_sum(o, i, axis=AX.X))
                    eng_inc("dve")
                    wait("act", sDVE)
                    act(SB["tmp16"][:], SB["ssq16"][:], AF.Ln, bias=SB["eps12"][:])
                    act(SB["rsq16"][:], SB["tmp16"][:], AF.Exp, scale=-0.5)
                    act_flush_inc(SB["rsq16"][:, 15:16])
                    wait("dve", sACT)
                    if t >= 2:
                        wait("dve", sPE, scr_free.get(("qsc", mode, cs), (sPE, 0))[1])
                    rsqB = SB["rsq16"].ap()[:, 0:16].rearrange(
                        "q (a b) -> q a b", b=1).broadcast_to([128, 16, 64])
                    em("dve", lambda e, o=qsc.rearrange("q (a b) -> q a b", b=64),
                       i=qrw.rearrange("q (a b) -> q a b", b=64), r=rsqB:
                       e.tensor_tensor(o, i, r, ALU.mult))
                    em("dve", lambda e, o=qsc, i=scale_rep[:]:
                       e.tensor_tensor(o, o, i, ALU.mult))
                    eng_inc("dve")
                    scr_free[("qraw", mode, cs)] = (sDVE, sDVE.n)
                    d_q = sDVE.n
                    if t + 1 < NT:
                        emit_mms(t + 1)
                    for g2 in range(2):
                        ti, tbank = get_tbank()
                        wait("pe", sDVE, d_q)
                        for j in range(4):
                            hp = g2 * 4 + j
                            trans(tbank[:, j * 128:(j + 1) * 128],
                                  qsc[:, hp * 128:(hp + 1) * 128])
                        eng_inc("pe")
                        wait("act", sPE)
                        dst = SB[dstT].ap().rearrange("q (p t f) -> q p t f", p=NP, t=NT)[
                            :, g2 * 4:(g2 + 1) * 4, t, :]
                        act(dst, tbank[:, 0:512].rearrange("q (a b) -> q a b", b=128),
                            AF.Copy)
                        eng_inc("act")
                        tbank_done(ti, sACT)
                    scr_free[("qsc", mode, cs)] = (sPE, sPE.n)
                scr_free["Wn2_readers"] = (sPE, last_mm_snap[0])

            qkv_gen(wq_d, "qTf", SB["qsr"], "q")
            qkv_gen(wk_d, "kTf", SB["ksr"], "k")
            for t in range(NT):
                em("pool", lambda e, o=chunks3("vf", NT, 16)[:, t, :, 64:65]:
                   e.memset(o, 1.0))
            eng_inc("pool")
            qkv_gen(wv_d, "vf", None, "v")
            return sPL.n

        def p2b(pl_vones):
            scr_state = [None] * 4
            slots = [PC[:, 0:512], PD[:, 0:512],
                     T0.ap().bitcast(F32), T1.ap().bitcast(F32)]
            sc_state = [None] * 4
            o_ps = [PA, PB]
            LOOK = 3
            wait("pe", sPL, pl_vones)
            wait("pe", sACT)
            for h in range(16):
                rb = (h % 2) * 64
                hp = h // 2
                head_dve_guard = sDVE.n
                hbase = hp * S
                wins = []
                for kc in range(NT):
                    for wi in range((NT - kc + 3) // 4):
                        tw = kc + wi * 4
                        wins.append((kc, wi, tw, min(4, NT - tw)))

                def emit_scores(i):
                    kc, wi, tw, ntile = wins[i]
                    si = i % 4
                    if sc_state[si] is not None:
                        wait("pe", *sc_state[si])
                    rhs = SB["qTf"][rb:rb + 64, hbase + tw * 128:hbase + tw * 128 + ntile * 128]
                    mm(slots[si][:, 0:ntile * 128],
                       SB["kTf"][rb:rb + 64, hbase + kc * 128:hbase + (kc + 1) * 128],
                       rhs, tp=(rb, 0))
                    eng_inc("pe")
                    return sPE.n

                pe_snap = {}
                emitted = [0]

                def ensure_scores(upto):
                    while emitted[0] <= min(upto, len(wins) - 1):
                        pe_snap[emitted[0]] = emit_scores(emitted[0])
                        emitted[0] += 1

                ensure_scores(1)
                for i0 in range(0, len(wins), 2):
                    pair = [i0] + ([i0 + 1] if i0 + 1 < len(wins) else [])
                    need_pe = []
                    for i in pair:
                        kc, wi, tw, ntile = wins[i]
                        si = i % 4
                        ci = i % 4
                        wait("act", sPE, pe_snap[i])
                        if scr_state[ci] is not None:
                            wait("act", *scr_state[ci])
                        scr = SB["scrF"][:, ci * 512:ci * 512 + ntile * 128]
                        act(scr, slots[si][:, 0:ntile * 128], AF.Exp, scale=10.0)
                        eng_inc("act")
                        sc_state[si] = (sACT, sACT.n)
                        a_exp = sACT.n
                        d_mask = None
                        if wi == 0:
                            wait("dve", sACT, a_exp)
                            em("dve", lambda e, o=SB["scrF"][:, ci * 512:ci * 512 + 128]:
                               e.tensor_tensor(o, o, SB["mask_tril"][:], ALU.mult))
                            eng_inc("dve")
                            d_mask = sDVE.n
                        need_pe.append((i, a_exp, d_mask))
                    ensure_scores(i0 + 1 + LOOK)
                    for (i, a_exp, d_mask) in need_pe:
                        kc, wi, tw, ntile = wins[i]
                        ci = i % 4
                        if d_mask is not None:
                            wait("pe", sDVE, d_mask)
                        else:
                            wait("pe", sACT, a_exp)
                        if i == 0:
                            wait("pe", sDVE, head_dve_guard)
                        for j in range(ntile):
                            qt = tw + j
                            ops = o_ps[qt // 8]
                            col = (qt % 8) * 128
                            mm(ops[:, col:col + 65],
                               SB["scrF"][:, ci * 512 + j * 128:ci * 512 + (j + 1) * 128],
                               SB["vf"][:, kc * 1040 + h * 65:kc * 1040 + (h + 1) * 65],
                               start=(kc == 0 and qt % 4 == 0), stop=(kc == qt), skip=True)
                        eng_inc("pe")
                        scr_state[ci] = (sPE, sPE.n)
                pe_head = sPE.n
                wait("dve", sPE, pe_head)
                for qt in range(NT):
                    ops = o_ps[qt // 8]
                    col = (qt % 8) * 128
                    em("dve", lambda e, o=st30(), i=ops[:, col + 64:col + 65]:
                       e.reciprocal(o, i))
                    dve_gap()
                    em("dve", lambda e, o=SB["osb_f"][:, qt * D + h * 64:qt * D + (h + 1) * 64],
                       i=ops[:, col:col + 64], r=st30():
                       e.tensor_scalar(o, i, r, None, ALU.mult))
                    eng_inc("dve")
                if h % 2 == 1:
                    d_pair = sDVE.n
                    for g2 in range(4):
                        ti = g2 % 2
                        tbank = [T0, T1][ti]
                        if sc_state[2 + ti] is not None:
                            wait("pe", *sc_state[2 + ti])
                        wait("pe", sDVE, d_pair)
                        for j in range(4):
                            qt = g2 * 4 + j
                            trans(tbank[:, j * 128:(j + 1) * 128],
                                  SB["osb_f"][:, qt * D + hp * 128:qt * D + (hp + 1) * 128])
                        eng_inc("pe")
                        wait("dve", sPE)
                        em("dve", lambda e,
                           o=SB["oTf"][:, hp * S + g2 * 512:hp * S + (g2 + 1) * 512],
                           i=tbank[:, 0:512]: e.tensor_copy(o, i))
                        eng_inc("dve")
                        sc_state[2 + ti] = (sDVE, sDVE.n)
                tb_free[0] = sc_state[2]
                tb_free[1] = sc_state[3]
            scr_free["p2b_pe_end"] = (sPE, sPE.n)

        def p3():
            wait("pool", sPE, scr_free.get("p2b_pe_end", (sPE, sPE.n))[1])
            g = pload(sbuf_nat("Wn3"), nat_ap(wo_d))
            wait("pe", sG, g)
            wait("pe", sACT)
            INV = 1.0 / D
            st = SB["stats"]
            p3_state = {}

            def emit_mms(t):
                ii = []
                for dg in range(2):
                    i, sl = get_slot()
                    ii.append((i, sl))
                    for pk in range(NP):
                        mm(sl, SB["oTf"][:, pk * S + t * 128:pk * S + (t + 1) * 128],
                           SB["Wn3"][:, pk * D + dg * 512:pk * D + (dg + 1) * 512],
                           start=(pk == 0), stop=(pk == NP - 1))
                    eng_inc("pe")
                p3_state[t] = (ii, sPE.n)

            emit_mms(0)
            for t in range(NT):
                ii, pe_q = p3_state.pop(t)
                cs = t % 2
                o2f = SB["out2f"][:, cs * D:(cs + 1) * D]
                h2s = SB["h2scr"][:, cs * D:(cs + 1) * D]
                wait("act", sPE, pe_q)
                if t >= 2 and ("o2f", cs) in scr_free:
                    wait("act", *scr_free[("o2f", cs)])
                for dg in range(2):
                    act(o2f[:, dg * 512:(dg + 1) * 512], ii[dg][1], AF.Copy)
                eng_inc("act")
                for dg in range(2):
                    slot_done(ii[dg][0], sACT)
                wait("dve", sACT)
                em("dve", lambda e, o=st[:, 0:1], i=o2f: e.reduce_sum(o, i, axis=AX.X))
                eng_inc("dve")
                act(h2s, o2f, AF.Square)
                eng_inc("act")
                wait("dve", sACT)
                em("dve", lambda e, o=st[:, 1:2], i=h2s: e.reduce_sum(o, i, axis=AX.X))
                em("dve", lambda e, o=st[:, 2:3], i=st[:, 0:1]:
                   e.tensor_scalar(o, i, INV, None, ALU.mult))
                em("dve", lambda e, o=st[:, 3:4], i=st[:, 1:2]:
                   e.tensor_scalar(o, i, INV, None, ALU.mult))
                em("dve", lambda e, o=st[:, 4:5], i=st[:, 2:3]:
                   e.tensor_tensor(o, i, i, ALU.mult))
                dve_gap()
                em("dve", lambda e, o=st[:, 5:6], a=st[:, 3:4], c=st[:, 4:5]:
                   e.tensor_tensor(o, a, c, ALU.subtract))
                eng_inc("dve")
                wait("act", sDVE)
                act(st[:, 6:7], st[:, 5:6], AF.Ln, bias=SB["eps5"][:])
                act(st[:, 7:8], st[:, 6:7], AF.Exp, scale=-0.5)
                act_flush_inc(st[:, 7:8])
                wait("dve", sACT)
                if t >= 2 and ("h2s", cs) in scr_free:
                    wait("dve", *scr_free[("h2s", cs)])
                em("dve", lambda e, o=h2s, i=o2f, m=st[:, 2:3], r=st[:, 7:8]:
                   e.tensor_scalar(o, i, m, r, ALU.subtract, ALU.mult))
                em("dve", lambda e, o=h2s:
                   e.tensor_tensor(o, o, SB["grep_b"][:], ALU.mult))
                em("dve", lambda e, o=h2s:
                   e.tensor_tensor(o, o, SB["brep_b"][:], ALU.add))
                eng_inc("dve")
                scr_free[("o2f", cs)] = (sDVE, sDVE.n)
                d_h2 = sDVE.n
                if t + 1 < NT:
                    emit_mms(t + 1)
                for g2 in range(2):
                    ti, tbank = get_tbank()
                    wait("pe", sDVE, d_h2)
                    for j in range(4):
                        p = g2 * 4 + j
                        trans(tbank[:, j * 128:(j + 1) * 128],
                              h2s[:, p * 128:(p + 1) * 128])
                    eng_inc("pe")
                    wait("act", sPE)
                    dst = chunks3("h2Tbuf", NP, NT)[:, g2 * 4:(g2 + 1) * 4, t, :]
                    act(dst, tbank[:, 0:512].rearrange("q (a b) -> q a b", b=128), AF.Copy)
                    eng_inc("act")
                    tbank_done(ti, sACT)
                scr_free[("h2s", cs)] = (sPE, sPE.n)
            wait("pe", sACT)
            for t in range(NT):
                for pk in range(NP):
                    mm(PC[:, t * 8:(t + 1) * 8],
                       SB["h2Tbuf"][:, pk * S + t * 128:pk * S + (t + 1) * 128],
                       SB["gw"][:, pk * 8:(pk + 1) * 8], start=(pk == 0), stop=False)
                mm(PC[:, t * 8:(t + 1) * 8], SB["onesr"][0:1, 0:128],
                   SB["gbr"][0:1, 0:8], start=False, stop=True, skip=True)
            eng_inc("pe")
            wait("dve", sPE)
            wait("dve", sACT)
            gx = SB["out2f"].ap()[:, 0:128]
            em("dve", lambda e, o=SB["ssq16"][:, 0:16],
               i=PC.ap()[:, 0:128].rearrange("q (a b) -> q a b", b=8):
               e.reduce_max(o, i, axis=AX.X))
            dve_gap()
            em("dve", lambda e, o=gx,
               i=PC.ap()[:, 0:128].rearrange("q (a b) -> q a b", b=8),
               m=SB["ssq16"].ap()[:, 0:16].rearrange("q (a b) -> q a b", b=1)
               .broadcast_to([128, 16, 8]):
               e.tensor_tensor(o.rearrange("q (a b) -> q a b", b=8), i, m, ALU.subtract))
            eng_inc("dve")
            wait("act", sDVE)
            act(gx, gx, AF.Exp)
            act_flush_inc(SB["out2f"][:, 0:1])
            wait("dve", sACT)
            em("dve", lambda e, o=SB["rsq16"][:, 0:16],
               i=gx.rearrange("q (a b) -> q a b", b=8):
               e.reduce_sum(o, i, axis=AX.X))
            dve_gap()
            em("dve", lambda e, o=SB["tmp16"][:, 0:16], i=SB["rsq16"][:, 0:16]:
               e.reciprocal(o, i))
            dve_gap()
            em("dve", lambda e, o=SB["gate"][:, 0:128], i=gx,
               m=SB["tmp16"].ap()[:, 0:16].rearrange("q (a b) -> q a b", b=1)
               .broadcast_to([128, 16, 8]):
               e.tensor_tensor(o.rearrange("q (a b) -> q a b", b=8),
                               i.rearrange("q (a b) -> q a b", b=8), m, ALU.mult))
            em("dve", lambda e, o=SB["h2scr"][:, 0:128], i=SB["gate"][:, 0:128]:
               e.tensor_copy(o, i))
            eng_inc("dve")
            for g2 in range(2):
                ti, tbank = get_tbank()
                wait("pe", sDVE)
                for j in range(8):
                    t = g2 * 8 + j
                    trans(tbank[0:8, j * 128:(j + 1) * 128],
                          SB["h2scr"][:, t * 8:(t + 1) * 8])
                eng_inc("pe")
                wait("act", sPE)
                act(SB["gateT"][0:8, g2 * 1024:(g2 + 1) * 1024], tbank[0:8, 0:1024],
                    AF.Copy)
                eng_inc("act")
                tbank_done(ti, sACT)
            wait("syn", sACT)
            wait("syn", sDVE)
            return sload(h2Tsp_d.ap(), SB["h2Tbuf"].ap())

        def p4(s_h2T):
            INV = 1.0 / D
            st = SB["stats"]
            a_re = A_d.ap().rearrange("(t q) f -> q t f", q=128)
            o_re = out_d.ap().rearrange("(t q) f -> q t f", q=128)
            w1_free = w2_free = None
            wait("pool", sPE)
            wait("pool", sDVE)
            wait("pool", sACT)
            wait("pool", sS, s_h2T)
            ph1_free = [None, None]
            ph2_free = [None, None]
            ph1_ps = [PA, PB]
            ph2_pairs = [(PC[:, 0:512], PD[:, 0:512]), (PA[:, 0:512], PA[:, 512:1024])]
            wait("syn", sS, s_h2T)
            wait("syn", sPE)
            wait("syn", sDVE)
            wait("syn", sACT)
            def epilogue_tile(tc, tt):
                g_t = tc * 8 + tt
                cs = tt % 2
                if scr_free.get(("epi", cs)):
                    wait("syn", *scr_free[("epi", cs)])
                    wait("act", *scr_free[("epi", cs)])
                    wait("dve", *scr_free[("epi", cs)])
                s_xa = sload(SB["xstg"].ap()[:, cs * D:(cs + 1) * D], a_re[:, g_t, :])
                moet = SB["moeA"][:, tt * D:(tt + 1) * D]
                em("dve", lambda e, o=st[:, 12:13], i=moet:
                   e.reduce_sum(o, i, axis=AX.X))
                eng_inc("dve")
                wait("act", sDVE)
                act(SB["ostg"][:, cs * D:(cs + 1) * D], moet, AF.Square)
                eng_inc("act")
                wait("dve", sACT)
                em("dve", lambda e, o=st[:, 13:14],
                   i=SB["ostg"][:, cs * D:(cs + 1) * D]:
                   e.reduce_sum(o, i, axis=AX.X))
                em("dve", lambda e, o=st[:, 14:15], i=st[:, 12:13]:
                   e.tensor_scalar(o, i, INV, None, ALU.mult))
                em("dve", lambda e, o=st[:, 15:16], i=st[:, 13:14]:
                   e.tensor_scalar(o, i, INV, None, ALU.mult))
                em("dve", lambda e, o=st[:, 16:17], i=st[:, 14:15]:
                   e.tensor_tensor(o, i, i, ALU.mult))
                dve_gap()
                em("dve", lambda e, o=st[:, 17:18], a=st[:, 15:16], c=st[:, 16:17]:
                   e.tensor_tensor(o, a, c, ALU.subtract))
                eng_inc("dve")
                wait("act", sDVE)
                act(st[:, 18:19], st[:, 17:18], AF.Ln, bias=SB["eps5"][:])
                act(st[:, 19:20], st[:, 18:19], AF.Exp, scale=-0.5)
                act_flush_inc(st[:, 19:20])
                wait("dve", sACT)
                wait("dve", sS, s_xa)
                ost = SB["ostg"][:, cs * D:(cs + 1) * D]
                em("dve", lambda e, o=ost, i=moet, m=st[:, 14:15], r=st[:, 19:20]:
                   e.tensor_scalar(o, i, m, r, ALU.subtract, ALU.mult))
                em("dve", lambda e, o=ost:
                   e.tensor_tensor(o, o, SB["grep_f"][:], ALU.mult))
                em("dve", lambda e, o=ost:
                   e.tensor_tensor(o, o, SB["brep_f"][:], ALU.add))
                em("dve", lambda e, o=ost, i=SB["xstg"][:, cs * D:(cs + 1) * D]:
                   e.tensor_tensor(o, o, i, ALU.add))
                eng_inc("dve")
                wait("syn", sDVE)
                em("syn", lambda e, d=o_re[:, g_t, :],
                   s=SB["ostg"].ap()[:, cs * D:(cs + 1) * D]:
                   e.dma_start(out=d, in_=s))
                inc("syn", sOUT, 16)
                scr_free[("epi", cs)] = (sOUT, sOUT.n)

            s_h2 = []
            for tc in range(2):
                s_h2.append(sload(
                    SB["h2Ttc"].ap()[:, tc * 8192:(tc + 1) * 8192]
                    .rearrange("q (p f) -> q p f", f=1024),
                    h2Tsp_d.ap().rearrange("q (p f) -> q p f", f=S)
                    [:, :, tc * 1024:(tc + 1) * 1024]))
            for tc in range(2):
                h2base = tc * 8192
                wait("pe", sG, g_const)
                wait("pe", sACT)
                for tt in range(8):
                    g_t = tc * 8 + tt
                    if ph2_free[0] is not None:
                        wait("pe", *ph2_free[0])
                    pL, pR = ph2_pairs[0]
                    mm(pL, SB["gateT"][0:8, g_t * 128:(g_t + 1) * 128],
                       SB["b2all"][0:8, 0:512])
                    mm(pR, SB["gateT"][0:8, g_t * 128:(g_t + 1) * 128],
                       SB["b2all"][0:8, 512:1024])
                    eng_inc("pe")
                    wait("dve", sPE)
                    for gg, psrc in ((0, pL), (1, pR)):
                        em("dve", lambda e, o=SB["moeA"][:, tt * D + gg * 512:
                                                         tt * D + (gg + 1) * 512],
                           i=psrc: e.tensor_copy(o, i))
                    eng_inc("dve")
                    ph2_free[0] = (sDVE, sDVE.n)
                for e_i in range(8):
                    for half in range(2):
                        if w1_free is not None:
                            wait("pool", *w1_free)
                        g_w1 = pload(chunks("w1h", 2048),
                                     w1_d.ap()[e_i, :, half * 2048:(half + 1) * 2048]
                                     .rearrange("(c q) f -> q c f", q=128))
                        if w2_free is not None:
                            wait("pool", *w2_free)
                        for piece in range(2):
                            g_w2 = pload(
                                SB["w2h"].ap()[:, piece * 8192:(piece + 1) * 8192]
                                .rearrange("q (c f) -> q c f", f=D),
                                w2_d.ap()[e_i, half * 2048 + piece * 1024:
                                          half * 2048 + (piece + 1) * 1024, :]
                                .rearrange("(c q) f -> q c f", q=128))
                        wait("pe", sG, g_w1)
                        wait("pe", sS, s_h2[tc])
                        for c in range(16):
                            pi = c % 2
                            if ph1_free[pi] is not None:
                                wait("pe", *ph1_free[pi])
                            for tg in range(2):
                                for pk in range(NP):
                                    mm(ph1_ps[pi][:, tg * 512:(tg + 1) * 512],
                                       SB["w1h"][:, pk * 2048 + c * 128:
                                                 pk * 2048 + (c + 1) * 128],
                                       SB["h2Ttc"][:, h2base + pk * 1024 + tg * 512:
                                                   h2base + pk * 1024 + (tg + 1) * 512],
                                       start=(pk == 0), stop=(pk == NP - 1))
                            eng_inc("pe")
                            wait("act", sPE)
                            bcol = SB["b1sb"][:, e_i * 32 + half * 16 + c:
                                              e_i * 32 + half * 16 + c + 1]
                            act(SB["hidT"][:, c * 1024:(c + 1) * 1024], ph1_ps[pi][:],
                                AF.Gelu_apprx_tanh, bias=bcol)
                            eng_inc("act")
                            ph1_free[pi] = (sACT, sACT.n)
                        w1_free = (sPE, sPE.n)
                        wait("pe", sG, g_w2)
                        wait("pe", sACT)
                        for tt in range(8):
                            pi = tt % 2
                            if ph2_free[pi] is not None:
                                wait("pe", *ph2_free[pi])
                            pL, pR = ph2_pairs[pi]
                            for c in range(16):
                                lhs = SB["hidT"][:, c * 1024 + tt * 128:
                                                 c * 1024 + (tt + 1) * 128]
                                stp = (c == 15)
                                mm(pL, lhs, SB["w2h"][:, c * 1024:c * 1024 + 512],
                                   start=(c == 0), stop=stp)
                                mm(pR, lhs, SB["w2h"][:, c * 1024 + 512:(c + 1) * 1024],
                                   start=(c == 0), stop=stp)
                            eng_inc("pe")
                            wait("dve", sPE)
                            gcol = SB["gate"][:, (tc * 8 + tt) * 8 + e_i:
                                              (tc * 8 + tt) * 8 + e_i + 1]
                            for gg, psrc in ((0, pL), (1, pR)):
                                dst = SB["moeA"][:, tt * D + gg * 512:tt * D + (gg + 1) * 512]
                                em("dve", lambda e, o=dst, i=psrc, r=gcol:
                                   e.scalar_tensor_tensor(o, i, r, o, ALU.mult, ALU.add))
                            eng_inc("dve")
                            ph2_free[pi] = (sDVE, sDVE.n)
                            if e_i == 7 and half == 1:
                                epilogue_tile(tc, tt)
                        w2_free = (sPE, sPE.n)
                        ph1_free[0] = (sDVE, sDVE.n)


        s_hT = p0()
        pl_vones = None
        if stage >= 1:
            barrier()
            p1(s_hT)
        if stage >= 2:
            barrier()
            pl_vones = p2a(s_hT)
        if stage >= 3:
            barrier()
            p2b(pl_vones)
            barrier()
            if "osb" in dbg:
                wait("syn", sACT)
                wait("syn", sDVE)
                em("syn", lambda e, d=dbg["osb"].ap(), s=SB["osb_f"].ap():
                   e.dma_start(out=d, in_=s))
                inc("syn", sOUT, 16)
                dbg.pop("osb")
            s_h2T = p3()
        if stage >= 4:
            barrier()
            p4(s_h2T)

        def dstore(name, sbname):
            if name in dbg:
                wait("syn", sACT)
                wait("syn", sDVE)
                em("syn", lambda e, d=dbg[name].ap(), s=SB[sbname].ap():
                   e.dma_start(out=d, in_=s))
                inc("syn", sOUT, 16)

        dstore("xnT", "xnT")
        dstore("qTf", "qTf")
        dstore("kTf", "kTf")
        dstore("vf", "vf")
        dstore("osb", "osb_f")
        dstore("gate", "gate")
        dstore("stats", "stats")
        dstore("moe", "moeA")

        wait("syn", sOUT, sOUT.n)
        wait("syn", sS, sS.n)
        wait("syn", sG, sG.n)

        for (q, sem, v) in WAITLOG:
            assert v <= sem.n, f"wait on {q} for {v} but sem tops out at {sem.n}"

        with nc.Block() as block:
            def runner(q):
                def go(eng):
                    for fn, incs in EM[q]:
                        r = fn(eng)
                        if incs is not None:
                            r.then_inc(*incs)
                return go
            block.tensor(runner("pe"))
            block.scalar(runner("act"))
            block.vector(runner("dve"))
            block.gpsimd(runner("pool"))
            block.sync(runner("syn"))

    return nc


def _prep(inputs):
    f32 = np.float32
    asf = lambda a: np.ascontiguousarray(np.asarray(a, f32))
    kq = np.arange(128)
    mask_attn = ((kq[:, None] // 4 == kq[None, :] // 4) &
                 (kq[:, None] != kq[None, :])).astype(f32)
    mask_tril = (kq[:, None] <= kq[None, :]).astype(f32)
    ipb = asf(inputs["L0_attn_in_proj_b"])
    shared = dict(
        pqk_w=asf(inputs["L0_attn_proj_qk_w"]),
        pv_w=asf(inputs["L0_attn_proj_v_w"]),
        ipw=asf(inputs["L0_attn_in_proj_w"]),
        opw=asf(inputs["L0_attn_out_proj_w"]),
        Wq=asf(inputs["L0_ffn_Wq"]),
        Wk=asf(inputs["L0_ffn_Wk"]),
        Wv=asf(inputs["L0_ffn_Wv"]),
        Wo=asf(inputs["L0_ffn_Wo"]),
        w1=asf(inputs["L0_ffn_w1"]),
        w2=asf(inputs["L0_ffn_w2"]),
        b2=asf(inputs["L0_ffn_b2"]),
        ident=np.eye(128, dtype=f32),
        mask_attn=mask_attn,
        mask_tril=mask_tril,
        eps5=np.full((128, 1), 1e-5, f32),
        eps12=np.full((128, 1), 1e-12, f32),
        qkb=np.ascontiguousarray(asf(inputs["L0_attn_proj_qk_b"]).reshape(8, 128).T),
        pvb=np.ascontiguousarray(asf(inputs["L0_attn_proj_v_b"]).reshape(8, 128).T),
        bqc=np.ascontiguousarray(ipb[0:1024].reshape(8, 128).T),
        bkc=np.ascontiguousarray(ipb[1024:2048].reshape(8, 128).T),
        b1r=np.ascontiguousarray(asf(inputs["L0_ffn_b1"]).reshape(8, 32, 128)
                                 .transpose(2, 0, 1).reshape(128, 256)),
        gwr=np.ascontiguousarray(asf(inputs["L0_ffn_gate_w"]).reshape(8, 128, 8)
                                 .transpose(1, 0, 2).reshape(128, 64)),
        grep=np.ascontiguousarray(np.broadcast_to(asf(inputs["L0_ffn_ln_g"]),
                                                  (128, 1024))),
        brep=np.ascontiguousarray(np.broadcast_to(asf(inputs["L0_ffn_ln_b"]),
                                                  (128, 1024))),
        qsr=np.ascontiguousarray(np.broadcast_to(
            asf(inputs["L0_ffn_q_scale"]).reshape(1, 1024), (128, 1024))),
        ksr=np.ascontiguousarray(np.broadcast_to(
            asf(inputs["L0_ffn_k_scale"]).reshape(1, 1024), (128, 1024))),
        onesr=np.ones((1, 128), f32),
        bvr=np.ascontiguousarray(ipb[2048:3072].reshape(1, 1024)),
        opbr=np.ascontiguousarray(asf(inputs["L0_attn_out_proj_b"]).reshape(1, 1024)),
        gbr=np.ascontiguousarray(asf(inputs["L0_ffn_gate_b"]).reshape(1, 8)),
    )
    x = asf(inputs["L0_attn_x"])
    return shared, x


def kernel(**inputs):
    shared, x = _prep(inputs)
    nc = build(stage=4)
    in_maps = [dict(shared, x=np.ascontiguousarray(x[i])) for i in range(NCORE)]
    res = run_bass_kernel_spmd(nc, in_maps, core_ids=list(range(NCORE)))
    return np.stack([np.asarray(res.results[i]["out"], np.float32)
                     for i in range(NCORE)], 0)
